# version 36
# baseline (speedup 1.0000x reference)
import contextlib
import os as _os
import numpy as np
import ml_dtypes
import concourse.bass as bass
import concourse.mybir as mybir
from concourse.bass_utils import run_bass_kernel_spmd

F32 = mybir.dt.float32
BF16 = mybir.dt.bfloat16
AF = mybir.ActivationFunctionType
ALU = mybir.AluOpType
AXX = mybir.AxisListType.X
RG8 = [[0, 1, 2, 3, 4, 5, 6, 7]]
MAGIC = 12582912.0
QMAX = 32767.0
EPS_LN = 1e-5
PI = float(np.pi)


class Sched:

    def __init__(self, nc):
        self.nc = nc
        self.ops = {k: [] for k in ("sync", "v", "s", "t", "g")}
        self.cnt = {k: 0 for k in ("vs", "ss", "ts", "gs")}
        self.sem = {}
        self.esem = {"v": "vs", "s": "ss", "t": "ts", "g": "gs"}

    def make_sems(self, stack):
        self.dma_ring = {"sync": [f"ld{i}" for i in range(6)],
                         "sd": [f"sd{i}" for i in range(4)]}
        self.dma_idx = {"sync": 0, "sd": 0}
        for q in self.dma_ring:
            for name in self.dma_ring[q]:
                self.cnt[name] = 0
        for name in list(self.cnt):
            self.sem[name] = stack.enter_context(self.nc.semaphore(name))

    def dma(self, queue, out, in_, **kw):
        stream = "sync" if queue == "sync" else "s"
        ring = self.dma_ring[queue]
        semname = ring[self.dma_idx[queue] % len(ring)]
        self.dma_idx[queue] += 1
        prev = self.cnt[semname]
        if prev:
            self.wait(stream, semname, prev)
        self.cnt[semname] = prev + 16
        tgt = self.cnt[semname]
        sem = self.sem[semname]

        def f(eng, out=out, in_=in_, kw=kw, sem=sem):
            eng.dma_start(out=out, in_=in_, **kw).then_inc(sem, 16)
        self.ops[stream].append(f)
        return (semname, tgt)

    def wtok(self, engine, token):
        if token:
            self.wait(engine, token[0], token[1])

    def op(self, engine, fname, *args, **kw):
        semname = self.esem[engine]
        prev = self.cnt[semname]
        if prev:
            self.wait(engine, semname, prev)
        self.cnt[semname] = prev + 1
        tgt = self.cnt[semname]
        sem = self.sem[semname]

        def f(eng, fname=fname, args=args, kw=kw, sem=sem):
            getattr(eng, fname)(*args, **kw).then_inc(sem, 1)
        self.ops[engine].append(f)
        return tgt

    def raw(self, engine, fname, *args, **kw):
        def f(eng, fname=fname, args=args, kw=kw):
            getattr(eng, fname)(*args, **kw)
        self.ops[engine].append(f)

    def fence(self, engine):
        return self.op(engine, "nop")

    def wait(self, engine, semname, value):
        if value <= 0:
            return
        stream = engine if engine in self.ops else "s"
        sem = self.sem[semname]

        def f(eng, sem=sem, value=value):
            eng.wait_ge(sem, value)
        self.ops[stream].append(f)

    def chain(self, engine, fname, *args, **kw):
        v = self.op(engine, fname, *args, **kw)
        self.wait(engine, self.esem[engine], v)
        return v

    def val(self, semname):
        return self.cnt[semname]


def build_nc(B=4, L=4096, D=1024, H=16, I=3072, n_cores=8, phases=99):
    DK = D // H
    S = L // n_cores
    T = B * S
    NT = T // 128
    SC = S // 128
    NJ = L // 128
    DC = D // 128
    IC = I // 128
    DSH = D // n_cores
    WDSH = I // n_cores
    assert S % 128 == 0 and DSH == 128 and D == 1024, "layout assumes D=1024"
    assert WDSH % 128 == 0, "w_down shard must be whole 128-row tiles"

    nc = bass.Bass("TRN2", target_bir_lowering=False, debug=False)

    x_in = nc.dram_tensor("x", [B, S, D], F32, kind="ExternalInput").ap()
    wup_sh = nc.dram_tensor("wup_sh", [I, DSH], F32, kind="ExternalInput").ap()
    wgate_sh = nc.dram_tensor("wgate_sh", [I, DSH], F32, kind="ExternalInput").ap()
    wdown_sh = nc.dram_tensor("wdown_sh", [D, WDSH], F32, kind="ExternalInput").ap()
    ident_bf = nc.dram_tensor("ident_bf", [128, 128], BF16, kind="ExternalInput").ap()
    ident_f = nc.dram_tensor("ident_f", [128, 128], F32, kind="ExternalInput").ap()
    y_out = nc.dram_tensor("y", [B, S, D], F32, kind="ExternalOutput").ap()

    dt_i = lambda n, sh, dt: nc.dram_tensor(n, sh, dt, kind="Internal").ap()
    mT_in = dt_i("mT_in", [DK, S], BF16)
    mT_g = dt_i("mT_g", [n_cores, DK, S], BF16)
    xln_in = [dt_i(f"xln_in{b}", [2, 128, S // 128, D // 2], BF16)
              for b in range(B)]
    xln_g = [dt_i(f"xln_g{b}", [n_cores, 2, 128, S // 128, D // 2], BF16)
             for b in range(B)]
    wupT_in = dt_i("wupT_in", [DSH, I], BF16)
    wupT_g = dt_i("wupT_g", [n_cores, DSH, I], BF16)
    wgateT_in = dt_i("wgateT_in", [DSH, I], BF16)
    wgateT_g = dt_i("wgateT_g", [n_cores, DSH, I], BF16)
    wdownT_in = dt_i("wdownT_in", [WDSH, D], BF16)
    wdownT_g = dt_i("wdownT_g", [n_cores, WDSH, D], BF16)
    s_rt = dt_i("s_rt", [1, S], F32)
    xq_d = dt_i("xq_d", [T, D], BF16)
    wqup_d = dt_i("wqup_d", [I, DSH], BF16)
    wqgate_d = dt_i("wqgate_d", [I, DSH], BF16)
    wqdown_d = dt_i("wqdown_d", [D, WDSH], BF16)
    up_sp = dt_i("up_sp", [I, T], F32)
    gate_sp = dt_i("gate_sp", [I, T], F32)
    tp_sp = dt_i("tp_sp", [I, T], F32)
    bars_in = [dt_i(f"bar{i}_in", [1, 2], F32) for i in range(4)]
    bars_out = [dt_i(f"bar{i}_out", [1, 2], F32) for i in range(4)]

    stack = contextlib.ExitStack()
    with stack:
        sb = lambda name, shape, dt: stack.enter_context(nc.sbuf_tensor(name, shape, dt)).ap() \
            if False else stack.enter_context(nc.sbuf_tensor(name, shape, dt))
        psal = lambda name, shape, dt: stack.enter_context(nc.psum_tensor(name, shape, dt))

        I8 = mybir.dt.int8
        ARENA_BYTES = max(131072 + IC * D * 2,
                          163840 + 8 * T,
                          174080 + SC * DK * 2,
                          65536 + NT * D * 4,
                          178688)
        arena = sb("arena", [128, ARENA_BYTES], I8)

        def av(off, n_elem, dt, shape=None):
            esz = 2 if dt == BF16 else 4
            v = arena[:, off:off + n_elem * esz].bitcast(dt)
            if shape is not None:
                v = v.rearrange("p (" + " ".join(f"a{i}" for i in range(len(shape)))
                                + ") -> p " + " ".join(f"a{i}" for i in range(len(shape))),
                                **{f"a{i}": s for i, s in enumerate(shape)})
            return v

        x_ring = av(0, 2 * D, F32, (2, D))
        xln_sb = av(8192, NT * D, BF16, (NT, D))
        dq_raw = lambda k, cols: av(167936 + k * cols * 4, cols, F32)
        dq_wq = lambda k, cols: av(171008 + k * cols * 2, cols, BF16)
        dq_wtr = lambda n: av(172544, n, BF16)
        u_half = av(65536, (NJ // 2) * S, F32)
        E_sb = av(131072, NJ * S, BF16, (NJ, S))
        mT_sb = av(163840, L, BF16)[0:DK, :]
        mT_stage = av(172032, S, BF16)[0:DK, :]
        mT_my = av(173056, S, BF16)[0:DK, :]
        m_bf = av(174080, SC * DK, BF16, (SC, DK))
        xatt = [av(0, NJ * 512, BF16, (NJ, 512)),
                av(32768, NJ * 512, BF16, (NJ, 512))]
        xatt4 = [av(0, NJ * 512, BF16, (n_cores, NJ // n_cores, 512)),
                 av(32768, NJ * 512, BF16, (n_cores, NJ // n_cores, 512))]
        xres = av(163840, 2 * 512, F32, (2, 512))
        y1 = av(65536, NT * D, F32, (NT, D))
        xqT = av(0, DC * T, BF16, (DC, T))
        xq_ring = av(32768, 2 * D, BF16, (2, D))
        wbuf = [av(36864, 2 * DC * 128, BF16, (2 * DC, 128)),
                av(36864 + DC * 512, 2 * DC * 128, BF16, (2 * DC, 128))]
        ug_stage = [av(131072, T, F32), av(131072 + T * 4, T, F32)]
        g_rd = [av(131072, T, F32), av(131072 + T * 4, T, F32),
                av(131072 + 2 * T * 4, T, F32)]
        u_rd = [av(155648, T, F32), av(155648 + T * 4, T, F32),
                av(155648 + 2 * T * 4, T, F32)]
        gq_sb = [av(0, T, F32), av(T * 4, T, F32), av(2 * T * 4, T, F32)]
        sg_sb = [av(24576, T, F32), av(24576 + T * 4, T, F32),
                 av(24576 + 2 * T * 4, T, F32)]
        tp_sb = [av(49152, T, F32), av(49152 + T * 4, T, F32)]
        wdT_sb = av(131072, IC * D, BF16, (IC, D))
        t2_rd = [av(0, IC * 128, F32, (IC, 128)),
                 av(IC * 512, IC * 128, F32, (IC, 128))]
        o_buf = [av(2 * IC * 512, IC * 128, BF16, (IC, 128)),
                 av(2 * IC * 512 + IC * 256, IC * 128, BF16, (IC, 128))]
        out_sb = av(36864, 2 * D, F32, (2, D))

        scr1 = sb("scr1", [128, D], F32)
        scr2 = sb("scr2", [128, D], F32)
        fscr = sb("fscr", [128, 1024], F32)
        stat = sb("stat", [128, 8], F32)
        m_acc = sb("m_acc", [128, SC, DK], F32)
        ones_bf = sb("ones_bf", [128, 1], BF16)
        ones_f = sb("ones_f", [1, 128], F32)
        id_bf = sb("id_bf", [128, 128], BF16)
        id_f = sb("id_f", [128, 128], F32)
        s_row = sb("s_row", [1, S], F32)
        recip_s = sb("recip_s", [128, SC], F32)
        absmax = sb("absmax", [128, 8], F32)
        bar_st = sb("bar_st", [1, 16], F32)
        scal_bc = sb("scal_bc", [128, 12], F32)
        wsca = sb("wsca", [128, 3 * (IC * 128) // 32], F32)
        ln2_mean = sb("ln2_mean", [128, NT], F32)
        ln2_isd = sb("ln2_isd", [128, NT], F32)
        sc_xq = sb("sc_xq", [128, NT], F32)

        pbank = [psal(f"pb{i}", [128, 512], F32) for i in range(6)]
        pbf = psal("pbf", [128, 1024], BF16)
        paux = psal("paux", [128, 512], F32)

        E = Sched(nc)
        E.make_sems(stack)

        def emit_blocks():
            with nc.Block() as block:
                @block.sync
                def _(sync):
                    for f in E.ops["sync"]:
                        f(sync)

                @block.scalar
                def _(scalar):
                    for f in E.ops["s"]:
                        f(scalar)

                @block.vector
                def _(vector):
                    for f in E.ops["v"]:
                        f(vector)

                @block.tensor
                def _(tensor):
                    for f in E.ops["t"]:
                        f(tensor)

                @block.gpsimd
                def _(gpsimd):
                    for f in E.ops["g"]:
                        f(gpsimd)


        E.op("v", "memset", ones_bf[:], 1.0)
        E.op("v", "memset", ones_f[:], 1.0)
        const_v = E.val("vs")
        ld_idf = E.dma("sync", id_f[:], ident_f)
        ld_idb = E.dma("sync", id_bf[:], ident_bf)

        def ln_tile(src_ap, dst_ap, prev_ssq, mean_ap=None, isd_ap=None):
            mean_ap = mean_ap if mean_ap is not None else stat[:, 1:2]
            isd_ap = isd_ap if isd_ap is not None else stat[:, 5:6]
            E.wait("s", "ss", prev_ssq)
            sum_s = E.op("s", "activation", scr2[:], src_ap, AF.Copy,
                         accum_out=stat[:, 0:1])
            E.wait("v", "ss", sum_s)
            E.chain("v", "tensor_scalar_mul", mean_ap, stat[:, 0:1], 1.0 / D)
            xc_s = E.chain("v", "tensor_scalar", scr1[:], src_ap,
                           mean_ap, None, ALU.subtract)
            E.wait("s", "vs", xc_s)
            ssq_s = E.op("s", "activation", scr2[:], scr1[:], AF.Square,
                         accum_out=stat[:, 2:3])
            E.wait("v", "ss", ssq_s)
            E.chain("v", "tensor_scalar", stat[:, 3:4], stat[:, 2:3], 1.0 / D,
                    EPS_LN, ALU.mult, ALU.add)
            E.wait("s", "vs", E.val("vs"))
            sq_s = E.op("s", "activation", stat[:, 4:5], stat[:, 3:4], AF.Sqrt)
            E.wait("v", "ss", sq_s)
            rc_v = E.chain("v", "reciprocal", isd_ap, stat[:, 4:5])
            E.wait("s", "vs", rc_v)
            out_s = E.op("s", "activation", dst_ap, scr1[:], AF.Copy,
                         scale=isd_ap)
            return xc_s, out_s, out_s

        x_t = x_in.rearrange("b (c p) d -> b c p d", p=128)
        ring_used = [0, 0]
        xln_stage_ld = []
        prev_ssq = 0
        fold_g_done = 0
        ln1_v = 0
        for t in range(NT):
            b, c = t // SC, t % SC
            slot = t % 2
            E.wait("sync", "vs", ring_used[slot])
            ld_v = E.dma("sync", x_ring[:, slot, :], x_t[b, c])
            E.wtok("v", ld_v)
            E.wtok("s", ld_v)
            xc_v, prev_ssq, xln_v = ln_tile(x_ring[:, slot, :], xln_sb[:, t, :], prev_ssq)
            ring_used[slot] = xc_v
            ln1_v = xln_v
            xv = xln_sb[:, t, :].rearrange("p (h k) -> p k h", k=DK)
            E.wait("v", "ss", xln_v)
            fold_g_done = E.chain("v", "reduce_sum",
                                  fscr[:, t * DK:(t + 1) * DK][:, :, None],
                                  xv, AXX)
            if c == SC - 1:
                E.wait("sync", "ss", xln_v)
                in_ld0 = E.dma("sync", xln_in[b][0],
                               xln_sb[:, b * SC:(b + 1) * SC, 0:512])
                in_ld1 = E.dma("sync", xln_in[b][1],
                               xln_sb[:, b * SC:(b + 1) * SC, 512:1024])
                xln_stage_ld.append((in_ld0, in_ld1))
        ln1_all_v = ln1_v

        E.wait("v", "vs", fold_g_done)
        fm = fscr[:, 0:NT * DK].rearrange("p (b z) -> p b z", b=B)
        E.chain("v", "tensor_tensor", fm[:, 0, :], fm[:, 0, :], fm[:, 2, :],
                ALU.add)
        E.chain("v", "tensor_tensor", fm[:, 1, :], fm[:, 1, :], fm[:, 3, :],
                ALU.add)
        E.chain("v", "tensor_tensor", m_acc[:].rearrange("p c k -> p (c k)"),
                fm[:, 0, :], fm[:, 1, :], ALU.add)
        mbf_v = E.chain("v", "tensor_scalar", m_bf[:], m_acc[:],
                        1.0 / (B * H), None, ALU.mult)
        E.wait("t", "vs", mbf_v)
        E.wtok("t", ld_idb)
        mT_cp_v = 0
        for c in range(SC):
            if c > 0:
                E.wait("t", "vs", mT_cp_v)
            tr_t = E.op("t", "transpose", pbf[0:DK, 0:128], m_bf[:, c, :], id_bf[:])
            E.wait("v", "ts", tr_t)
            mT_cp_v = E.op("v", "tensor_copy", mT_stage[:, c * 128:(c + 1) * 128],
                           pbf[0:DK, 0:128])
        E.wait("sync", "vs", mT_cp_v)
        mT_st = E.dma("sync", mT_in, mT_stage[:])

        if phases <= 2:
            dummy = E.dma("sync", y_out.rearrange("b (c p) d -> b c p d", p=128)[0, 0],
                          scr1[:])
            E.wtok("sync", dummy)
            emit_blocks()
            return nc

        xln_gath_g = []
        mT_gath_g = None
        for b in range(B):
            E.wtok("g", xln_stage_ld[b][0])
            E.wtok("g", xln_stage_ld[b][1])
            gg = E.op("g", "collective_compute", "AllGather", ALU.bypass,
                      RG8, [xln_in[b]], [xln_g[b]])
            xln_gath_g.append(gg)
            if b == 0:
                E.wtok("g", mT_st)
                mT_gath_g = E.op("g", "collective_compute", "AllGather",
                                 ALU.bypass, RG8, [mT_in], [mT_g])
        E.wait("sync", "gs", mT_gath_g)
        mT_rd = E.dma("sync", mT_sb[:].rearrange("k (r s) -> k r s", r=n_cores),
                      mT_g.rearrange("r k s -> k r s"))
        my_rd = E.dma("sync", mT_my[:], mT_in)

        E.wtok("t", mT_rd)
        E.wtok("t", my_rd)
        NJH = NJ // 2
        clip_v = {}
        exp_s = 0
        for half in range(2):
            for jj in range(NJH):
                jc = half * NJH + jj
                bank = pbank[jc % 2]
                if jc >= 2:
                    E.wait("t", "vs", clip_v[jc - 2])
                mm_t = E.op("t", "matmul", bank[:, 0:S],
                            mT_sb[:, jc * 128:(jc + 1) * 128],
                            mT_my[:], start=True, stop=True)
                E.wait("v", "ts", mm_t)
                if jj == 0:
                    E.wait("v", "ss", exp_s)
                clip_v[jc] = E.op("v", "tensor_scalar",
                                  u_half[:, jj * S:(jj + 1) * S],
                                  bank[:, 0:S], PI, -PI, ALU.min, ALU.max)
            E.wait("s", "vs", clip_v[half * NJH + NJH - 1])
            sin_s = E.op("s", "activation", u_half[:], u_half[:], AF.Sin, scale=0.5)
            E.wait("s", "ss", sin_s)
            sq2_s = E.op("s", "activation", u_half[:], u_half[:], AF.Square)
            E.wait("s", "ss", sq2_s)
            exp_s = E.op("s", "activation",
                         E_sb[:, half * NJH:(half + 1) * NJH, :].rearrange(
                             "p j s -> p (j s)"),
                         u_half[:], AF.Exp)

        E.wait("t", "ss", exp_s)
        for jc in range(NJ - 1):
            E.raw("t", "matmul", paux[0:1, 0:S], ones_bf[:], E_sb[:, jc, :],
                  start=(jc == 0), stop=False)
        ssum_t = E.op("t", "matmul", paux[0:1, 0:S], ones_bf[:],
                      E_sb[:, NJ - 1, :], start=(NJ == 1), stop=True)
        E.wait("v", "ts", ssum_t)
        srow_v = E.chain("v", "tensor_copy", s_row[:], paux[0:1, 0:S])
        E.wait("t", "vs", srow_v)
        tr_v = None
        for c in range(SC):
            tr_v = E.op("t", "matmul", paux[:, 256 + c:257 + c],
                        s_row[0:1, c * 128:(c + 1) * 128], ones_f[0:1, 0:1],
                        start=True, stop=True)
        E.wait("v", "ts", tr_v)
        rsv = E.chain("v", "reciprocal", recip_s[:], paux[:, 256:256 + SC])

        QW = 7.0
        dq_state = {"prev_used_v": 0, "prev_wq": [], "prev_stage": None,
                    "ts_guard": None}
        dq_ag = {}

        def mk_dq(name, w_ap, n_rt, cols, wq_d_ap, wT_in_ap, g_in, g_out):
            nblk = cols // 32
            nct = cols // 128
            rows = n_rt * 128
            R = 4 if cols == 128 else 2
            raw_sl = [dq_raw(k, cols) for k in range(R)]
            wq_sl = [dq_wq(k, cols) for k in range(2)]
            st = {"ld": {}, "used_v": {}, "wq_st": {}}

            def issue_ld(j):
                if j >= n_rt:
                    return
                if j - R in st["used_v"]:
                    E.wait("s", "vs", st["used_v"][j - R])
                elif j < R:
                    E.wait("s", "vs", dq_state["prev_used_v"])
                st["ld"][j] = E.dma("sd", raw_sl[j % R],
                                    w_ap[j * 128:(j + 1) * 128, :])

            def unit(rt):
                if rt == 0:
                    for j in range(min(R, n_rt)):
                        issue_ld(j)
                else:
                    issue_ld(rt + R - 1)
                wv = raw_sl[rt % R].rearrange("p (n k) -> p n k", k=32)
                E.wtok("v", st["ld"][rt])
                E.chain("v", "tensor_reduce", wsca[:, 0:nblk][:, :, None], wv,
                        AXX, ALU.max, apply_absolute_value=True)
                E.chain("v", "tensor_scalar", wsca[:, nblk:2 * nblk],
                        wsca[:, 0:nblk], 1.0 / QW, 1e-12, ALU.mult, ALU.max)
                E.chain("v", "reciprocal", wsca[:, 2 * nblk:3 * nblk],
                        wsca[:, nblk:2 * nblk])
                rs_b = wsca[:, 2 * nblk:3 * nblk][:, :, None].broadcast_to(
                    [128, nblk, 32])
                s_b = wsca[:, nblk:2 * nblk][:, :, None].broadcast_to(
                    [128, nblk, 32])
                used = E.chain("v", "tensor_tensor",
                               fscr[:, 0:cols].rearrange("p (n k) -> p n k", k=32),
                               wv, rs_b, ALU.mult)
                st["used_v"][rt] = used
                E.chain("v", "tensor_scalar", fscr[:, 0:cols], fscr[:, 0:cols],
                        MAGIC, MAGIC, ALU.add, ALU.subtract)
                E.chain("v", "tensor_scalar", fscr[:, 0:cols], fscr[:, 0:cols],
                        QW, -QW - 1.0, ALU.min, ALU.max)
                if rt - 2 in st["wq_st"]:
                    E.wtok("v", st["wq_st"][rt - 2])
                elif rt < 2:
                    for tok in dq_state["prev_wq"]:
                        E.wtok("v", tok)
                wq_v = E.chain("v", "tensor_tensor",
                               wq_sl[rt % 2].rearrange("p (n k) -> p n k", k=32),
                               fscr[:, 0:cols].rearrange("p (n k) -> p n k", k=32),
                               s_b, ALU.mult)
                E.wait("sync", "vs", wq_v)
                st["wq_st"][rt] = E.dma(
                    "sync", wq_d_ap[rt * 128:(rt + 1) * 128, :], wq_sl[rt % 2])
                if rt == n_rt - 1:
                    dq_state["prev_used_v"] = used
                    dq_state["prev_wq"] = [st["wq_st"][n_rt - 2],
                                           st["wq_st"][n_rt - 1]]

            def finish():
                for tok in st["wq_st"].values():
                    E.wtok("sync", tok)
                E.wtok("sync", dq_state["prev_stage"])
                if dq_state["ts_guard"] is not None:
                    E.wait("sync", "ts", dq_state["ts_guard"])
                    dq_state["ts_guard"] = None
                wtr = dq_wtr(nct * rows)
                tr_lds = [E.dma("sync", wtr[:, ct * rows:(ct + 1) * rows],
                                wq_d_ap[:, ct * 128:(ct + 1) * 128],
                                transpose=True) for ct in range(nct)]
                for tok in tr_lds:
                    E.wtok("sync", tok)
                if nct == 1:
                    stage = E.dma("sync", wT_in_ap, wtr[:, 0:rows])
                else:
                    stage = E.dma("sync",
                                  wT_in_ap.rearrange("(ct p) d -> p ct d", p=128),
                                  wtr[:, 0:nct * rows].rearrange(
                                      "p (ct r) -> p ct r", ct=nct))
                dq_state["prev_stage"] = stage
                dq_ag[name + "_st"] = stage
                if name != "down":
                    E.wtok("g", stage)
                    dq_ag[name] = E.op("g", "collective_compute", "AllGather",
                                       ALU.bypass, RG8, [g_in], [g_out])

            return [lambda rt=rt: unit(rt) for rt in range(n_rt)] + [finish]

        dq_units = (mk_dq("up", wup_sh, IC, 128, wqup_d, wupT_in,
                          wupT_in, wupT_g)
                    + mk_dq("gate", wgate_sh, IC, 128, wqgate_d, wgateT_in,
                            wgateT_in, wgateT_g)
                    + mk_dq("down", wdown_sh, DC, WDSH, wqdown_d, wdownT_in,
                            wdownT_in, wdownT_g))
        dq_pos = [0]

        def drain_dq(n):
            k = 0
            while dq_pos[0] < len(dq_units) and k < n:
                dq_units[dq_pos[0]]()
                dq_pos[0] += 1
                k += 1

        if phases <= 3:
            dummy = E.dma("sync", y_out.rearrange("b (c p) d -> b c p d", p=128)[0, 0],
                          scr1[:])
            E.wtok("sync", dummy)
            emit_blocks()
            return nc

        if phases <= 4:
            dummy = E.dma("sync", y_out.rearrange("b (c p) d -> b c p d", p=128)[0, 0],
                          scr1[:])
            E.wtok("sync", dummy)
            emit_blocks()
            return nc

        xres_t = x_in.rearrange("b (c p) d -> b c p d", p=128)
        xatt_ld = {}
        batch_mm_done = {}
        att_ev = {}
        ln2_prev_ssq = [prev_ssq]
        ln2_red_prev = [0]
        dq_state["ts_guard"] = E.val("ts")
        dh0_done = {}
        for b in range(B):
            if b >= 1:
                E.wait("sync", "ts", dh0_done[b - 1])
            E.wait("sync", "gs", xln_gath_g[b])
            src = xln_g[b].rearrange("r e p c d -> e p r c d")
            RC = n_cores // 4
            xatt_ld[(b, 0)] = [
                E.dma("sync", xatt4[0][:, k * RC:(k + 1) * RC],
                      src[0][:, k * RC:(k + 1) * RC]) for k in range(4)]
            if b >= 1:
                E.wait("sync", "ts", batch_mm_done[b - 1])
            xatt_ld[(b, 1)] = [
                E.dma("sync", xatt4[1][:, k * RC:(k + 1) * RC],
                      src[1][:, k * RC:(k + 1) * RC]) for k in range(4)]
            for dh in range(2):
                buf = xatt[dh]
                k = b * 2 + dh
                for i in range(SC):
                    gi = k * SC + i
                    bank = pbank[2 + gi % 4]
                    if gi >= 4:
                        E.wait("t", "vs", att_ev[gi - 4])
                    if i > 0:
                        E.wtok("t", xatt_ld[(b, dh)][3])
                    for jc in range(NJ - 1):
                        if i == 0 and jc % (NJ // 4) == 0:
                            E.wtok("t", xatt_ld[(b, dh)][jc // (NJ // 4)])
                        E.raw("t", "matmul", bank[:], E_sb[:, jc, i * 128:(i + 1) * 128],
                              buf[:, jc, :], start=(jc == 0), stop=False)
                    mm_t = E.op("t", "matmul", bank[:],
                                E_sb[:, NJ - 1, i * 128:(i + 1) * 128],
                                buf[:, NJ - 1, :], start=(NJ == 1), stop=True)
                    slot = gi % 2
                    if gi >= 2:
                        E.wait("s", "vs", att_ev[gi - 2])
                    xr_ld = E.dma("sd", xres[:, slot, :],
                                  xres_t[b, i][:, dh * 512:(dh + 1) * 512])
                    E.wait("v", "ts", mm_t)
                    E.wtok("v", xr_ld)
                    if gi == 0:
                        E.wait("v", "vs", rsv)
                    att_ev[gi] = E.op("v", "scalar_tensor_tensor",
                                      y1[:, b * SC + i, dh * 512:(dh + 1) * 512],
                                      bank[:], recip_s[:, i:i + 1], xres[:, slot, :],
                                      ALU.mult, ALU.add)
                    drain_dq(3)
                if dh == 0:
                    dh0_done[b] = E.val("ts")
            batch_mm_done[b] = E.val("ts")
            for c2 in range(SC):
                t = b * SC + c2
                E.wait("v", "vs", att_ev[(b * 2 + 1) * SC + c2])
                E.wait("s", "vs", att_ev[(b * 2 + 1) * SC + c2])
                E.wait("s", "vs", ln2_red_prev[0])
                _, ln2_prev_ssq[0], x2_v = ln_tile(
                    y1[:, t, :], scr2[:], ln2_prev_ssq[0],
                    mean_ap=ln2_mean[:, t:t + 1], isd_ap=ln2_isd[:, t:t + 1])
                E.wait("v", "ss", x2_v)
                ln2_red_prev[0] = E.chain("v", "tensor_reduce", absmax[:, 1:2],
                                          scr2[:], AXX, ALU.max,
                                          apply_absolute_value=True)
                if t == 0:
                    E.chain("v", "tensor_copy", absmax[:, 0:1], absmax[:, 1:2])
                else:
                    E.chain("v", "tensor_tensor", absmax[:, 0:1],
                            absmax[:, 0:1], absmax[:, 1:2], ALU.max)
        attn_all_v = att_ev[2 * B * SC - 1]
        drain_dq(999)

        if phases <= 5:
            dummy = E.dma("sync", y_out.rearrange("b (c p) d -> b c p d", p=128)[0, 0],
                          scr1[:])
            E.wtok("sync", dummy)
            emit_blocks()
            return nc

        ln2_all_v = E.val("vs")

        def barrier_maxes(cols, bar_in_ap, bar_out_ap):
            n = len(cols)
            E.wait("t", "vs", E.val("vs"))
            E.wtok("t", ld_idf)
            rd_v = 0
            for ii, cc in enumerate(cols):
                if ii > 0:
                    E.wait("t", "vs", rd_v)
                tr = E.op("t", "transpose", paux[0:1, 0:128],
                          absmax[:, cc:cc + 1], id_f[:])
                E.wait("v", "ts", tr)
                rd_v = E.chain("v", "reduce_max", bar_st[:, 8 + ii:9 + ii],
                               paux[0:1, 0:128], AXX)
            E.wait("sync", "vs", rd_v)
            st = E.dma("sync", bar_in_ap[:, 0:n], bar_st[:, 8:8 + n])
            E.wtok("g", st)
            gg = E.op("g", "collective_compute", "AllReduce", ALU.max,
                      RG8, [bar_in_ap[:, 0:n]], [bar_out_ap[:, 0:n]])
            E.wait("sync", "gs", gg)
            return E.dma("sync", bar_st[:, 0:n], bar_out_ap[:, 0:n])

        def bcast(src_col, dst_col, n):
            E.wait("t", "vs", E.val("vs"))
            mmv = E.op("t", "matmul", paux[:, 0:n], ones_f[:],
                       bar_st[:, src_col:src_col + n], start=True, stop=True)
            E.wait("v", "ts", mmv)
            return E.chain("v", "tensor_copy", scal_bc[:, dst_col:dst_col + n],
                           paux[:, 0:n])

        b1_rd = barrier_maxes([0], bars_in[0], bars_out[0])
        E.wtok("g", dq_ag["down_st"])
        dq_ag["down"] = E.op("g", "collective_compute", "AllGather",
                             ALU.bypass, RG8, [wdownT_in], [wdownT_g])
        E.wtok("v", b1_rd)
        E.chain("v", "tensor_scalar", bar_st[:, 2:3], bar_st[:, 0:1],
                1.0 / QMAX, 1e-12, ALU.mult, ALU.max)
        E.chain("v", "reciprocal", bar_st[:, 3:4], bar_st[:, 2:3])
        bc1_v = bcast(2, 0, 2)

        E.wait("v", "vs", bc1_v)
        E.wait("v", "ss", E.val("ss"))
        E.chain("v", "tensor_tensor", sc_xq[:], ln2_isd[:],
                scal_bc[:, 1:2].broadcast_to([128, NT]), ALU.mult)
        E.chain("v", "tensor_tensor", ln2_mean[:], ln2_mean[:], sc_xq[:],
                ALU.mult)
        pre_v = E.chain("v", "tensor_scalar", ln2_mean[:], ln2_mean[:],
                        -1.0, MAGIC, ALU.mult, ALU.add)
        xq_d_t = xq_d.rearrange("(t p) d -> t p d", p=128)
        xq_tile_st = {}
        xq_s = {}
        xq_vv = {}
        E.wait("s", "vs", pre_v)
        for t in range(NT):
            buf = scr1 if t % 2 == 0 else scr2
            if t >= 2:
                E.wait("s", "vs", xq_vv[t - 2])
            xq_s[t] = E.op("s", "activation", buf[:], y1[:, t, :],
                           AF.Identity, scale=sc_xq[:, t:t + 1],
                           bias=ln2_mean[:, t:t + 1])
            E.wait("v", "ss", xq_s[t])
            slot = t % 2
            if t >= 2:
                E.wtok("v", xq_tile_st[t - 2])
            xq_v = E.chain("v", "tensor_scalar", xq_ring[:, slot, :], buf[:],
                           MAGIC, scal_bc[:, 0:1], ALU.subtract, ALU.mult)
            xq_vv[t] = xq_v
            E.wait("sync", "vs", xq_v)
            xq_tile_st[t] = E.dma("sync", xq_d_t[t], xq_ring[:, slot, :])
        for t in range(NT):
            E.wtok("sync", xq_tile_st[t])
        xqT_lds = []
        for dc in range(DC):
            xqT_lds.append(E.dma("sync", xqT[:, dc, :],
                                 xq_d[:, dc * 128:(dc + 1) * 128], transpose=True))

        if phases <= 6:
            dummy = E.dma("sync", y_out.rearrange("b (c p) d -> b c p d", p=128)[0, 0],
                          scr1[:])
            E.wtok("sync", dummy)
            emit_blocks()
            return nc

        up_spT = up_sp.rearrange("(ic p) t -> ic p t", p=128)
        gate_spT = gate_sp.rearrange("(ic p) t -> ic p t", p=128)
        tp_spT = tp_sp.rearrange("(ic p) t -> ic p t", p=128)
        wupT_v = wupT_g.rearrange("r p i -> p r i")
        wgateT_v = wgateT_g.rearrange("r p i -> p r i")
        ug_spill = {}
        wb_mm_done = {}
        first_max = True
        E.wait("sync", "gs", dq_ag["up"])
        E.wait("sync", "gs", dq_ag["gate"])
        ev_s = 0
        ev_v = 0
        last_red_v = 0
        sp_done_for_stage = None
        _P10N = int(_os.environ.get("P10N", "999"))
        for ic in range(min(IC, _P10N)):
            wb = wbuf[ic % 2]
            if ic >= 2:
                E.wait("sync", "ts", wb_mm_done[ic - 2])
            l1 = E.dma("sync", wb[:, 0:DC, :],
                       wupT_v[:, :, ic * 128:(ic + 1) * 128])
            l2 = E.dma("sync", wb[:, DC:2 * DC, :],
                       wgateT_v[:, :, ic * 128:(ic + 1) * 128])
            E.wtok("t", l1)
            E.wtok("t", l2)
            if ic == 0:
                for tok in xqT_lds:
                    E.wtok("t", tok)
            NQ = T // 512
            for h0 in range(0, NQ, 2):
                tqs = list(range(h0, min(h0 + 2, NQ)))
                E.wait("t", "ss", ev_s)
                E.wait("t", "vs", ev_v)
                mm_t = None
                for dc in range(DC):
                    for q, tc in enumerate(tqs):
                        last = (dc == DC - 1) and (q == len(tqs) - 1)
                        E.raw("t", "matmul", pbank[q][:], wb[:, dc, :],
                              xqT[:, dc, tc * 512:(tc + 1) * 512],
                              start=(dc == 0), stop=(dc == DC - 1))
                        if not last:
                            E.raw("t", "matmul", pbank[2 + q][:], wb[:, DC + dc, :],
                                  xqT[:, dc, tc * 512:(tc + 1) * 512],
                                  start=(dc == 0), stop=(dc == DC - 1))
                        else:
                            mm_t = E.op("t", "matmul", pbank[2 + q][:],
                                        wb[:, DC + dc, :],
                                        xqT[:, dc, tc * 512:(tc + 1) * 512],
                                        start=(dc == 0), stop=True)
                E.wait("s", "ts", mm_t)
                E.wait("v", "ts", mm_t)
                if _os.environ.get("P10EVAC", "1") == "0":
                    ev_s = E.val("ss"); ev_v = E.val("vs")
                    continue
                if sp_done_for_stage:
                    E.wtok("s", sp_done_for_stage[0])
                    E.wtok("s", sp_done_for_stage[1])
                    E.wtok("v", sp_done_for_stage[0])
                    E.wtok("v", sp_done_for_stage[1])
                E.wait("s", "vs", last_red_v)
                for q, tc in enumerate(tqs):
                    E.op("s", "activation", ug_stage[0][:, tc * 512:(tc + 1) * 512],
                         pbank[q][:], AF.Copy)
                    E.op("v", "tensor_copy", ug_stage[1][:, tc * 512:(tc + 1) * 512],
                         pbank[2 + q][:])
                ev_s = E.val("ss")
                ev_v = E.val("vs")
            E.wait("v", "ss", ev_s)
            E.chain("v", "tensor_reduce", wsca[:, ic:ic + 1], ug_stage[0][:],
                    AXX, ALU.max, apply_absolute_value=True)
            last_red_v = E.chain("v", "tensor_reduce", wsca[:, IC + ic:IC + ic + 1],
                                 ug_stage[1][:], AXX, ALU.max,
                                 apply_absolute_value=True)
            wb_mm_done[ic] = E.val("ts")
            if _os.environ.get("P10SPILL", "1") == "0":
                ug_spill[ic] = None
                continue
            E.wait("s", "ss", ev_s)
            s1d = E.dma("sd", up_spT[ic], ug_stage[0][:])
            s2d = E.dma("sd", gate_spT[ic], ug_stage[1][:])
            ug_spill[ic] = (s1d, s2d)
            sp_done_for_stage = (s1d, s2d)

        if _os.environ.get("P10B", "1") == "0":
            dummy = E.dma("sync", y_out.rearrange("b (c p) d -> b c p d", p=128)[0, 0],
                          scr1[:])
            E.wtok("sync", dummy)
            emit_blocks()
            return nc
        E.chain("v", "tensor_reduce", absmax[:, 2:3], wsca[:, 0:IC],
                AXX, ALU.max)
        E.chain("v", "tensor_reduce", absmax[:, 3:4], wsca[:, IC:2 * IC],
                AXX, ALU.max)
        tp_st = {}
        p1_ss = {}
        k3_vv = {}
        sig_ss = {}
        stt_vv = {}
        g_lds = {}

        def f5_ld(j):
            if j >= IC:
                return
            E.wtok("sync", ug_spill[j][1])
            if j >= 3:
                E.wait("sync", "ss", p1_ss[j - 3])
            g_lds[j] = E.dma("sync", g_rd[j % 3][:], gate_spT[j])

        for _ic in range(max(0, IC - 4), IC):
            E.wtok("sync", ug_spill[_ic][0])
            E.wtok("sync", ug_spill[_ic][1])
        E.wait("sync", "vs", E.val("vs"))
        for j in range(3):
            f5_ld(j)
        b2_rd = barrier_maxes([2, 3], bars_in[1], bars_out[1])
        E.wtok("v", b2_rd)
        E.chain("v", "tensor_scalar", bar_st[:, 2:3], bar_st[:, 0:1],
                1.0 / QMAX, 1e-12, ALU.mult, ALU.max)
        E.chain("v", "reciprocal", bar_st[:, 3:4], bar_st[:, 2:3])
        E.chain("v", "tensor_scalar", bar_st[:, 4:5], bar_st[:, 1:2],
                1.0 / QMAX, 1e-12, ALU.mult, ALU.max)
        E.chain("v", "reciprocal", bar_st[:, 5:6], bar_st[:, 4:5])
        E.chain("v", "tensor_scalar_mul", bar_st[:, 6:7], bar_st[:, 4:5],
                -MAGIC)
        bc2_v = bcast(2, 2, 5)

        if phases <= 7:
            dummy = E.dma("sync", y_out.rearrange("b (c p) d -> b c p d", p=128)[0, 0],
                          scr1[:])
            E.wtok("sync", dummy)
            emit_blocks()
            return nc

        E.wait("v", "vs", bc2_v)
        E.wait("s", "vs", bc2_v)
        for it in range(IC + 2):
            if it < IC:
                ic = it
                rb = g_rd[ic % 3]
                gq = gq_sb[ic % 3]
                E.wtok("s", g_lds[ic])
                if ic >= 3:
                    E.wait("s", "vs", stt_vv[ic - 3])
                p1_ss[ic] = E.op("s", "activation", gq[:], rb[:], AF.Copy,
                                 scale=scal_bc[:, 5:6], bias=MAGIC)
                f5_ld(ic + 3)
            if 1 <= it < IC + 1:
                ic = it - 1
                gq = gq_sb[ic % 3]
                sg = sg_sb[ic % 3]
                if ic >= 3:
                    E.wait("s", "vs", stt_vv[ic - 3])
                E.op("s", "activation", sg[:], gq[:], AF.Sigmoid,
                     scale=scal_bc[:, 4:5], bias=scal_bc[:, 6:7])
                sig_ss[ic] = E.op("s", "activation", sg[:], sg[:], AF.Copy,
                                  scale=65536.0, bias=MAGIC)
            if it >= 2:
                ic = it - 2
                gq = gq_sb[ic % 3]
                sg = sg_sb[ic % 3]
                tb = tp_sb[ic % 2]
                E.wait("v", "ss", sig_ss[ic])
                E.chain("v", "tensor_scalar", sg[:], sg[:],
                        MAGIC, 65535.0, ALU.subtract, ALU.min)
                if ic >= 2:
                    E.wtok("v", tp_st[ic - 2])
                stt_vv[ic] = E.chain("v", "scalar_tensor_tensor", tb[:],
                                     gq[:], MAGIC, sg[:],
                                     ALU.subtract, ALU.mult)
                E.chain("v", "tensor_reduce", wsca[:, 2 * IC + ic:2 * IC + ic + 1],
                        tb[:], AXX, ALU.max, apply_absolute_value=True)
                E.wait("sync", "vs", stt_vv[ic])
                tp_st[ic] = E.dma("sync", tp_spT[ic], tb[:])

        E.chain("v", "tensor_reduce", absmax[:, 4:5], wsca[:, 2 * IC:3 * IC],
                AXX, ALU.max)
        t2_st = {}
        p1b_ss = {}
        p2b_ss = {}
        k4_vv = {}
        q2b_gg = {}
        stt2_vv = {}
        tp_lds = {}
        up_lds = {}

        def f7_ld(j):
            if j >= IC:
                return
            E.wtok("sync", tp_st[j])
            E.wtok("sync", ug_spill[j][0])
            if j == 0:
                E.wtok("sync", dq_ag["down_st"])
            if j >= 3:
                E.wait("sync", "ss", p1b_ss[j - 3])
            else:
                E.wait("sync", "ss", p1_ss[IC - 3 + j])
            tp_lds[j] = E.dma("sync", g_rd[j % 3][:], tp_spT[j])
            if j >= 3:
                E.wait("sync", "ss", p2b_ss[j - 3])
            up_lds[j] = E.dma("sync", u_rd[j % 3][:], up_spT[j])

        for j in range(3):
            f7_ld(j)
        b3_rd = barrier_maxes([4], bars_in[2], bars_out[2])
        E.wtok("v", b3_rd)
        E.chain("v", "tensor_scalar", bar_st[:, 7:8], bar_st[:, 0:1],
                bar_st[0:1, 4:5], 1.0 / 65536.0 / QMAX, ALU.mult, ALU.mult)
        E.chain("v", "tensor_scalar_max", bar_st[:, 7:8], bar_st[:, 7:8],
                1e-12)
        E.chain("v", "reciprocal", bar_st[:, 3:4], bar_st[:, 7:8])
        E.chain("v", "tensor_scalar", bar_st[:, 3:4], bar_st[:, 3:4],
                bar_st[0:1, 4:5], 1.0 / 65536.0, ALU.mult, ALU.mult)
        bc3_v = bcast(3, 7, 1)

        if phases <= 8:
            dummy = E.dma("sync", y_out.rearrange("b (c p) d -> b c p d", p=128)[0, 0],
                          scr1[:])
            E.wtok("sync", dummy)
            emit_blocks()
            return nc

        E.wait("v", "vs", bc3_v)
        E.wait("s", "vs", bc3_v)
        for it in range(IC + 1):
            if it < IC:
                ic = it
                rb = g_rd[ic % 3]
                ub = u_rd[ic % 3]
                gq = gq_sb[ic % 3]
                sg = sg_sb[ic % 3]
                E.wtok("s", tp_lds[ic])
                if ic >= 3:
                    E.wait("s", "vs", stt2_vv[ic - 3])
                p1b_ss[ic] = E.op("s", "activation", gq[:], rb[:], AF.Copy,
                                  scale=scal_bc[:, 7:8], bias=MAGIC)
                E.wtok("s", up_lds[ic])
                if ic >= 3:
                    E.wait("s", "vs", stt2_vv[ic - 3])
                p2b_ss[ic] = E.op("s", "activation", sg[:], ub[:], AF.Copy,
                                  scale=scal_bc[:, 3:4], bias=MAGIC)
                f7_ld(ic + 3)
                E.wait("v", "ss", p1b_ss[ic])
                k4_vv[ic] = E.chain("v", "tensor_scalar", gq[:], gq[:],
                                    MAGIC, None, ALU.subtract)
            if it >= 1:
                ic = it - 1
                gq = gq_sb[ic % 3]
                sg = sg_sb[ic % 3]
                tb = tp_sb[ic % 2]
                E.wait("v", "ss", p2b_ss[ic])
                if ic >= 2:
                    E.wtok("v", t2_st[ic - 2])
                stt2_vv[ic] = E.chain("v", "scalar_tensor_tensor", tb[:],
                                      sg[:], MAGIC, gq[:],
                                      ALU.subtract, ALU.mult)
                E.chain("v", "tensor_reduce",
                        wsca[:, 3 * IC + ic:3 * IC + ic + 1],
                        tb[:], AXX, ALU.max, apply_absolute_value=True)
                E.wait("sync", "vs", stt2_vv[ic])
                t2_st[ic] = E.dma("sync", gate_spT[ic], tb[:])

        E.chain("v", "tensor_reduce", absmax[:, 6:7], wsca[:, 3 * IC:4 * IC],
                AXX, ALU.max)
        t2_view = gate_sp.rearrange("(ic p) t -> p ic t", p=128)
        t2_lds = {}
        o_v = {}

        def p16_ld(j):
            if j >= NT:
                return
            if j == 0:
                for _tok in t2_st.values():
                    E.wtok("sync", _tok)
            if j >= 2:
                E.wait("sync", "vs", o_v[j - 2])
            t2_lds[j] = E.dma("sync", t2_rd[j % 2][:],
                              t2_view[:, :, j * 128:(j + 1) * 128])

        for j in range(2):
            p16_ld(j)
        b4_rd = barrier_maxes([6], bars_in[3], bars_out[3])
        E.wtok("v", b4_rd)
        E.chain("v", "tensor_scalar", bar_st[:, 6:7], bar_st[:, 7:8],
                bar_st[0:1, 2:3], None, ALU.mult)
        E.chain("v", "tensor_scalar", bar_st[:, 4:5], bar_st[:, 0:1],
                bar_st[0:1, 6:7], 1.0 / QMAX, ALU.mult, ALU.mult)
        E.chain("v", "tensor_scalar_max", bar_st[:, 4:5], bar_st[:, 4:5],
                1e-12)
        E.chain("v", "reciprocal", bar_st[:, 5:6], bar_st[:, 4:5])
        E.chain("v", "tensor_scalar", bar_st[:, 5:6], bar_st[:, 5:6],
                bar_st[0:1, 6:7], None, ALU.mult)
        bc4_v = bcast(4, 8, 2)

        E.wait("sync", "gs", dq_ag["down"])
        wdT_ld = E.dma("sync", wdT_sb[:],
                       wdownT_g.rearrange("r w d -> (r w) d").rearrange(
                           "(ic p) d -> p ic d", p=128))

        if phases <= 9:
            dummy = E.dma("sync", y_out.rearrange("b (c p) d -> b c p d", p=128)[0, 0],
                          scr1[:])
            E.wtok("sync", dummy)
            emit_blocks()
            return nc

        dn_done = {}
        fin_done = {}
        out_st = {}
        E.wait("v", "vs", bc4_v)
        E.wait("s", "vs", bc4_v)
        for tc in range(NT + 1):
            if tc < NT:
                rb = t2_rd[tc % 2]
                ob = o_buf[tc % 2]
                rv = rb[:].rearrange("p a q -> p (a q)")
                ov = ob[:].rearrange("p a q -> p (a q)")
                E.wtok("s", t2_lds[tc])
                if tc >= 2:
                    E.wait("s", "vs", o_v[tc - 2])
                p1_s = E.op("s", "activation", rv, rv, AF.Copy,
                            scale=scal_bc[:, 9:10], bias=MAGIC)
                E.wait("v", "ss", p1_s)
                o_v[tc] = E.chain("v", "tensor_scalar", ov, rv,
                                  MAGIC, scal_bc[:, 8:9], ALU.subtract, ALU.mult)
                p16_ld(tc + 2)
                E.wait("t", "vs", o_v[tc])
                E.wtok("t", wdT_ld)
                bk = 2 * (tc % 2)
                if tc >= 2:
                    E.wait("t", "vs", fin_done[tc - 2])
                dn_t = None
                for d2c in range(2):
                    for icx in range(IC):
                        last = (d2c == 1) and (icx == IC - 1)
                        if not last:
                            E.raw("t", "matmul", pbank[bk + d2c][:], ob[:, icx, :],
                                  wdT_sb[:, icx, d2c * 512:(d2c + 1) * 512],
                                  start=(icx == 0), stop=(icx == IC - 1))
                        else:
                            dn_t = E.op("t", "matmul", pbank[bk + d2c][:],
                                        ob[:, icx, :],
                                        wdT_sb[:, icx, d2c * 512:(d2c + 1) * 512],
                                        start=(icx == 0), stop=True)
                dn_done[tc] = dn_t
            if tc >= 1:
                tp = tc - 1
                bkp = 2 * (tp % 2)
                slot = tp % 2
                E.wait("v", "ts", dn_done[tp])
                if tp >= 2:
                    E.wtok("v", out_st[tp - 2])
                E.chain("v", "tensor_add", out_sb[:, slot, 0:512], pbank[bkp][:],
                        y1[:, tp, 0:512])
                fin_done[tp] = E.chain("v", "tensor_add",
                                       out_sb[:, slot, 512:1024],
                                       pbank[bkp + 1][:], y1[:, tp, 512:1024])
                b, c = tp // SC, tp % SC
                E.wait("sync", "vs", fin_done[tp])
                out_st[tp] = E.dma("sync",
                                   y_out.rearrange("b (c p) d -> b c p d", p=128)[b, c],
                                   out_sb[:, slot, :])

        E.wtok("sync", out_st[NT - 1])

        emit_blocks()
        return nc
    return nc



def kernel(L0_attn_x, L0_ffn_w_gate, L0_ffn_w_up, L0_ffn_w_down):
    x = np.asarray(L0_attn_x, dtype=np.float32)
    wg = np.asarray(L0_ffn_w_gate, dtype=np.float32)
    wu = np.asarray(L0_ffn_w_up, dtype=np.float32)
    wd = np.asarray(L0_ffn_w_down, dtype=np.float32)
    B, L, D = x.shape
    I = wu.shape[0]
    n = 8
    S = L // n
    nc = build_nc(B=B, L=L, D=D, H=16, I=I, n_cores=n)
    ident_bf = np.eye(128, dtype=ml_dtypes.bfloat16)
    ident_f = np.eye(128, dtype=np.float32)
    in_maps = []
    for c in range(n):
        in_maps.append({
            "x": np.ascontiguousarray(x[:, S * c:S * (c + 1), :]),
            "wup_sh": np.ascontiguousarray(wu[:, (D // n) * c:(D // n) * (c + 1)]),
            "wgate_sh": np.ascontiguousarray(wg[:, (D // n) * c:(D // n) * (c + 1)]),
            "wdown_sh": np.ascontiguousarray(wd[:, (I // n) * c:(I // n) * (c + 1)]),
            "ident_bf": ident_bf,
            "ident_f": ident_f,
        })
    res = run_bass_kernel_spmd(nc, in_maps, core_ids=list(range(n)))
    return np.concatenate([res.results[c]["y"] for c in range(n)], axis=1)


# version 37
# speedup vs baseline: 1.0574x; 1.0574x over previous
import contextlib
import os as _os
import numpy as np
import ml_dtypes
import concourse.bass as bass
import concourse.mybir as mybir
from concourse.bass_utils import run_bass_kernel_spmd

F32 = mybir.dt.float32
BF16 = mybir.dt.bfloat16
AF = mybir.ActivationFunctionType
ALU = mybir.AluOpType
AXX = mybir.AxisListType.X
RG8 = [[0, 1, 2, 3, 4, 5, 6, 7]]
MAGIC = 12582912.0
QMAX = 32767.0
EPS_LN = 1e-5
PI = float(np.pi)


class Sched:

    def __init__(self, nc):
        self.nc = nc
        self.ops = {k: [] for k in ("sync", "v", "s", "t", "g")}
        self.cnt = {k: 0 for k in ("vs", "ss", "ts", "gs")}
        self.sem = {}
        self.esem = {"v": "vs", "s": "ss", "t": "ts", "g": "gs"}

    def make_sems(self, stack):
        self.dma_ring = {"sync": [f"ld{i}" for i in range(6)],
                         "sd": [f"sd{i}" for i in range(4)]}
        self.dma_idx = {"sync": 0, "sd": 0}
        for q in self.dma_ring:
            for name in self.dma_ring[q]:
                self.cnt[name] = 0
        for name in list(self.cnt):
            self.sem[name] = stack.enter_context(self.nc.semaphore(name))

    def dma(self, queue, out, in_, **kw):
        stream = "sync" if queue == "sync" else "s"
        ring = self.dma_ring[queue]
        semname = ring[self.dma_idx[queue] % len(ring)]
        self.dma_idx[queue] += 1
        prev = self.cnt[semname]
        if prev:
            self.wait(stream, semname, prev)
        self.cnt[semname] = prev + 16
        tgt = self.cnt[semname]
        sem = self.sem[semname]

        def f(eng, out=out, in_=in_, kw=kw, sem=sem):
            eng.dma_start(out=out, in_=in_, **kw).then_inc(sem, 16)
        self.ops[stream].append(f)
        return (semname, tgt)

    def wtok(self, engine, token):
        if token:
            self.wait(engine, token[0], token[1])

    def op(self, engine, fname, *args, **kw):
        semname = self.esem[engine]
        prev = self.cnt[semname]
        if prev:
            self.wait(engine, semname, prev)
        self.cnt[semname] = prev + 1
        tgt = self.cnt[semname]
        sem = self.sem[semname]

        def f(eng, fname=fname, args=args, kw=kw, sem=sem):
            getattr(eng, fname)(*args, **kw).then_inc(sem, 1)
        self.ops[engine].append(f)
        return tgt

    def raw(self, engine, fname, *args, **kw):
        def f(eng, fname=fname, args=args, kw=kw):
            getattr(eng, fname)(*args, **kw)
        self.ops[engine].append(f)

    def fence(self, engine):
        return self.op(engine, "nop")

    def wait(self, engine, semname, value):
        if value <= 0:
            return
        stream = engine if engine in self.ops else "s"
        sem = self.sem[semname]

        def f(eng, sem=sem, value=value):
            eng.wait_ge(sem, value)
        self.ops[stream].append(f)

    def chain(self, engine, fname, *args, **kw):
        v = self.op(engine, fname, *args, **kw)
        self.wait(engine, self.esem[engine], v)
        return v

    def val(self, semname):
        return self.cnt[semname]


def build_nc(B=4, L=4096, D=1024, H=16, I=3072, n_cores=8, phases=99):
    DK = D // H
    S = L // n_cores
    T = B * S
    NT = T // 128
    SC = S // 128
    NJ = L // 128
    DC = D // 128
    IC = I // 128
    DSH = D // n_cores
    WDSH = I // n_cores
    assert S % 128 == 0 and DSH == 128 and D == 1024, "layout assumes D=1024"
    assert WDSH % 128 == 0, "w_down shard must be whole 128-row tiles"

    nc = bass.Bass("TRN2", target_bir_lowering=False, debug=False)

    x_in = nc.dram_tensor("x", [B, S, D], F32, kind="ExternalInput").ap()
    wup_sh = nc.dram_tensor("wup_sh", [I, DSH], F32, kind="ExternalInput").ap()
    wgate_sh = nc.dram_tensor("wgate_sh", [I, DSH], F32, kind="ExternalInput").ap()
    wdown_sh = nc.dram_tensor("wdown_sh", [D, WDSH], F32, kind="ExternalInput").ap()
    ident_bf = nc.dram_tensor("ident_bf", [128, 128], BF16, kind="ExternalInput").ap()
    ident_f = nc.dram_tensor("ident_f", [128, 128], F32, kind="ExternalInput").ap()
    y_out = nc.dram_tensor("y", [B, S, D], F32, kind="ExternalOutput").ap()

    dt_i = lambda n, sh, dt: nc.dram_tensor(n, sh, dt, kind="Internal").ap()
    mT_in = dt_i("mT_in", [DK, S], BF16)
    mT_g = dt_i("mT_g", [n_cores, DK, S], BF16)
    xln_in = [dt_i(f"xln_in{b}", [2, 128, S // 128, D // 2], BF16)
              for b in range(B)]
    xln_g = [dt_i(f"xln_g{b}", [n_cores, 2, 128, S // 128, D // 2], BF16)
             for b in range(B)]
    wupT_in = dt_i("wupT_in", [DSH, I], BF16)
    wupT_g = dt_i("wupT_g", [n_cores, DSH, I], BF16)
    wgateT_in = dt_i("wgateT_in", [DSH, I], BF16)
    wgateT_g = dt_i("wgateT_g", [n_cores, DSH, I], BF16)
    wdownT_in = dt_i("wdownT_in", [WDSH, D], BF16)
    wdownT_g = dt_i("wdownT_g", [n_cores, WDSH, D], BF16)
    s_rt = dt_i("s_rt", [1, S], F32)
    xq_d = dt_i("xq_d", [T, D], BF16)
    wqup_d = dt_i("wqup_d", [I, DSH], BF16)
    wqgate_d = dt_i("wqgate_d", [I, DSH], BF16)
    wqdown_d = dt_i("wqdown_d", [D, WDSH], BF16)
    up_sp = dt_i("up_sp", [I, T], F32)
    gate_sp = dt_i("gate_sp", [I, T], F32)
    tp_sp = dt_i("tp_sp", [I, T], F32)
    bars_in = [dt_i(f"bar{i}_in", [1, 2], F32) for i in range(4)]
    bars_out = [dt_i(f"bar{i}_out", [1, 2], F32) for i in range(4)]

    stack = contextlib.ExitStack()
    with stack:
        sb = lambda name, shape, dt: stack.enter_context(nc.sbuf_tensor(name, shape, dt)).ap() \
            if False else stack.enter_context(nc.sbuf_tensor(name, shape, dt))
        psal = lambda name, shape, dt: stack.enter_context(nc.psum_tensor(name, shape, dt))

        I8 = mybir.dt.int8
        ARENA_BYTES = max(131072 + IC * D * 2,
                          163840 + 8 * T,
                          174080 + SC * DK * 2,
                          65536 + NT * D * 4,
                          178688)
        arena = sb("arena", [128, ARENA_BYTES], I8)

        def av(off, n_elem, dt, shape=None):
            esz = 2 if dt == BF16 else 4
            v = arena[:, off:off + n_elem * esz].bitcast(dt)
            if shape is not None:
                v = v.rearrange("p (" + " ".join(f"a{i}" for i in range(len(shape)))
                                + ") -> p " + " ".join(f"a{i}" for i in range(len(shape))),
                                **{f"a{i}": s for i, s in enumerate(shape)})
            return v

        x_ring = av(0, 2 * D, F32, (2, D))
        xln_sb = av(8192, NT * D, BF16, (NT, D))
        dq_raw = lambda k, cols: av(167936 + k * cols * 4, cols, F32)
        dq_wq = lambda k, cols: av(171008 + k * cols * 2, cols, BF16)
        dq_wtr = lambda n: av(172544, n, BF16)
        u_half = av(65536, (NJ // 2) * S, F32)
        E_sb = av(131072, NJ * S, BF16, (NJ, S))
        mT_sb = av(163840, L, BF16)[0:DK, :]
        mT_stage = av(172032, S, BF16)[0:DK, :]
        mT_my = av(173056, S, BF16)[0:DK, :]
        m_bf = av(174080, SC * DK, BF16, (SC, DK))
        xatt = [av(0, NJ * 512, BF16, (NJ, 512)),
                av(32768, NJ * 512, BF16, (NJ, 512))]
        xatt4 = [av(0, NJ * 512, BF16, (n_cores, NJ // n_cores, 512)),
                 av(32768, NJ * 512, BF16, (n_cores, NJ // n_cores, 512))]
        xres = av(163840, 2 * 512, F32, (2, 512))
        y1 = av(65536, NT * D, F32, (NT, D))
        xqT = av(0, DC * T, BF16, (DC, T))
        xq_ring = av(32768, 2 * D, BF16, (2, D))
        wbuf = [av(36864, 2 * DC * 128, BF16, (2 * DC, 128)),
                av(36864 + DC * 512, 2 * DC * 128, BF16, (2 * DC, 128))]
        ug_stage = [av(131072, T, F32), av(131072 + T * 4, T, F32)]
        g_rd = [av(131072, T, F32), av(131072 + T * 4, T, F32),
                av(131072 + 2 * T * 4, T, F32)]
        u_rd = [av(155648, T, F32), av(155648 + T * 4, T, F32),
                av(155648 + 2 * T * 4, T, F32)]
        gq_sb = [av(0, T, F32), av(T * 4, T, F32), av(2 * T * 4, T, F32)]
        sg_sb = [av(24576, T, F32), av(24576 + T * 4, T, F32),
                 av(24576 + 2 * T * 4, T, F32)]
        tp_sb = [av(49152, T, F32), av(49152 + T * 4, T, F32)]
        wdT_sb = av(131072, IC * D, BF16, (IC, D))
        t2_rd = [av(0, IC * 128, F32, (IC, 128)),
                 av(IC * 512, IC * 128, F32, (IC, 128))]
        o_buf = [av(2 * IC * 512, IC * 128, BF16, (IC, 128)),
                 av(2 * IC * 512 + IC * 256, IC * 128, BF16, (IC, 128))]
        out_sb = av(36864, 2 * D, F32, (2, D))

        scr1 = sb("scr1", [128, D], F32)
        scr2 = sb("scr2", [128, D], F32)
        fscr = sb("fscr", [128, 1024], F32)
        stat = sb("stat", [128, 8], F32)
        m_acc = sb("m_acc", [128, SC, DK], F32)
        ones_bf = sb("ones_bf", [128, 1], BF16)
        ones_f = sb("ones_f", [1, 128], F32)
        id_bf = sb("id_bf", [128, 128], BF16)
        id_f = sb("id_f", [128, 128], F32)
        s_row = sb("s_row", [1, S], F32)
        recip_s = sb("recip_s", [128, SC], F32)
        absmax = sb("absmax", [128, 8], F32)
        bar_st = sb("bar_st", [1, 16], F32)
        scal_bc = sb("scal_bc", [128, 12], F32)
        wsca = sb("wsca", [128, 3 * (IC * 128) // 32], F32)
        ln2_mean = sb("ln2_mean", [128, NT], F32)
        ln2_isd = sb("ln2_isd", [128, NT], F32)
        sc_xq = sb("sc_xq", [128, NT], F32)

        pbank = [psal(f"pb{i}", [128, 512], F32) for i in range(6)]
        pbf = psal("pbf", [128, 1024], BF16)
        paux = psal("paux", [128, 512], F32)

        E = Sched(nc)
        E.make_sems(stack)

        def emit_blocks():
            with nc.Block() as block:
                @block.sync
                def _(sync):
                    for f in E.ops["sync"]:
                        f(sync)

                @block.scalar
                def _(scalar):
                    for f in E.ops["s"]:
                        f(scalar)

                @block.vector
                def _(vector):
                    for f in E.ops["v"]:
                        f(vector)

                @block.tensor
                def _(tensor):
                    for f in E.ops["t"]:
                        f(tensor)

                @block.gpsimd
                def _(gpsimd):
                    for f in E.ops["g"]:
                        f(gpsimd)


        E.op("v", "memset", ones_bf[:], 1.0)
        E.op("v", "memset", ones_f[:], 1.0)
        const_v = E.val("vs")
        ld_idf = E.dma("sync", id_f[:], ident_f)
        ld_idb = E.dma("sync", id_bf[:], ident_bf)

        def ln_tile(src_ap, dst_ap, prev_ssq, mean_ap=None, isd_ap=None):
            mean_ap = mean_ap if mean_ap is not None else stat[:, 1:2]
            isd_ap = isd_ap if isd_ap is not None else stat[:, 5:6]
            E.wait("s", "ss", prev_ssq)
            sum_s = E.op("s", "activation", scr2[:], src_ap, AF.Copy,
                         accum_out=stat[:, 0:1])
            E.wait("v", "ss", sum_s)
            E.chain("v", "tensor_scalar_mul", mean_ap, stat[:, 0:1], 1.0 / D)
            xc_s = E.chain("v", "tensor_scalar", scr1[:], src_ap,
                           mean_ap, None, ALU.subtract)
            E.wait("s", "vs", xc_s)
            ssq_s = E.op("s", "activation", scr2[:], scr1[:], AF.Square,
                         accum_out=stat[:, 2:3])
            E.wait("v", "ss", ssq_s)
            E.chain("v", "tensor_scalar", stat[:, 3:4], stat[:, 2:3], 1.0 / D,
                    EPS_LN, ALU.mult, ALU.add)
            E.wait("s", "vs", E.val("vs"))
            sq_s = E.op("s", "activation", stat[:, 4:5], stat[:, 3:4], AF.Sqrt)
            E.wait("v", "ss", sq_s)
            rc_v = E.chain("v", "reciprocal", isd_ap, stat[:, 4:5])
            E.wait("s", "vs", rc_v)
            out_s = E.op("s", "activation", dst_ap, scr1[:], AF.Copy,
                         scale=isd_ap)
            return xc_s, out_s, out_s

        x_t = x_in.rearrange("b (c p) d -> b c p d", p=128)
        ring_used = [0, 0]
        xln_stage_ld = []
        prev_ssq = 0
        fold_g_done = 0
        ln1_v = 0
        for t in range(NT):
            b, c = t // SC, t % SC
            slot = t % 2
            E.wait("sync", "vs", ring_used[slot])
            ld_v = E.dma("sync", x_ring[:, slot, :], x_t[b, c])
            E.wtok("v", ld_v)
            E.wtok("s", ld_v)
            xc_v, prev_ssq, xln_v = ln_tile(x_ring[:, slot, :], xln_sb[:, t, :], prev_ssq)
            ring_used[slot] = xc_v
            ln1_v = xln_v
            xv = xln_sb[:, t, :].rearrange("p (h k) -> p k h", k=DK)
            E.wait("v", "ss", xln_v)
            fold_g_done = E.chain("v", "reduce_sum",
                                  fscr[:, t * DK:(t + 1) * DK][:, :, None],
                                  xv, AXX)
            if c == SC - 1:
                E.wait("sync", "ss", xln_v)
                in_ld0 = E.dma("sync", xln_in[b][0],
                               xln_sb[:, b * SC:(b + 1) * SC, 0:512])
                in_ld1 = E.dma("sync", xln_in[b][1],
                               xln_sb[:, b * SC:(b + 1) * SC, 512:1024])
                xln_stage_ld.append((in_ld0, in_ld1))
        ln1_all_v = ln1_v

        E.wait("v", "vs", fold_g_done)
        fm = fscr[:, 0:NT * DK].rearrange("p (b z) -> p b z", b=B)
        E.chain("v", "tensor_tensor", fm[:, 0, :], fm[:, 0, :], fm[:, 2, :],
                ALU.add)
        E.chain("v", "tensor_tensor", fm[:, 1, :], fm[:, 1, :], fm[:, 3, :],
                ALU.add)
        E.chain("v", "tensor_tensor", m_acc[:].rearrange("p c k -> p (c k)"),
                fm[:, 0, :], fm[:, 1, :], ALU.add)
        mbf_v = E.chain("v", "tensor_scalar", m_bf[:], m_acc[:],
                        1.0 / (B * H), None, ALU.mult)
        E.wait("t", "vs", mbf_v)
        E.wtok("t", ld_idb)
        mT_cp_v = 0
        for c in range(SC):
            if c > 0:
                E.wait("t", "vs", mT_cp_v)
            tr_t = E.op("t", "transpose", pbf[0:DK, 0:128], m_bf[:, c, :], id_bf[:])
            E.wait("v", "ts", tr_t)
            mT_cp_v = E.op("v", "tensor_copy", mT_stage[:, c * 128:(c + 1) * 128],
                           pbf[0:DK, 0:128])
        E.wait("sync", "vs", mT_cp_v)
        mT_st = E.dma("sync", mT_in, mT_stage[:])

        if phases <= 2:
            dummy = E.dma("sync", y_out.rearrange("b (c p) d -> b c p d", p=128)[0, 0],
                          scr1[:])
            E.wtok("sync", dummy)
            emit_blocks()
            return nc

        xln_gath_g = []
        mT_gath_g = None
        for b in range(B):
            E.wtok("g", xln_stage_ld[b][0])
            E.wtok("g", xln_stage_ld[b][1])
            gg = E.op("g", "collective_compute", "AllGather", ALU.bypass,
                      RG8, [xln_in[b]], [xln_g[b]])
            xln_gath_g.append(gg)
            if b == 0:
                E.wtok("g", mT_st)
                mT_gath_g = E.op("g", "collective_compute", "AllGather",
                                 ALU.bypass, RG8, [mT_in], [mT_g])
        E.wait("sync", "gs", mT_gath_g)
        mT_rd = E.dma("sync", mT_sb[:].rearrange("k (r s) -> k r s", r=n_cores),
                      mT_g.rearrange("r k s -> k r s"))
        my_rd = E.dma("sync", mT_my[:], mT_in)

        E.wtok("t", mT_rd)
        E.wtok("t", my_rd)
        NJH = NJ // 2
        clip_v = {}
        exp_s = 0
        for half in range(2):
            for jj in range(NJH):
                jc = half * NJH + jj
                bank = pbank[jc % 2]
                if jc >= 2:
                    E.wait("t", "vs", clip_v[jc - 2])
                mm_t = E.op("t", "matmul", bank[:, 0:S],
                            mT_sb[:, jc * 128:(jc + 1) * 128],
                            mT_my[:], start=True, stop=True)
                E.wait("v", "ts", mm_t)
                if jj == 0:
                    E.wait("v", "ss", exp_s)
                clip_v[jc] = E.op("v", "tensor_scalar",
                                  u_half[:, jj * S:(jj + 1) * S],
                                  bank[:, 0:S], PI, -PI, ALU.min, ALU.max)
            E.wait("s", "vs", clip_v[half * NJH + NJH - 1])
            sin_s = E.op("s", "activation", u_half[:], u_half[:], AF.Sin, scale=0.5)
            E.wait("s", "ss", sin_s)
            sq2_s = E.op("s", "activation", u_half[:], u_half[:], AF.Square)
            E.wait("s", "ss", sq2_s)
            exp_s = E.op("s", "activation",
                         E_sb[:, half * NJH:(half + 1) * NJH, :].rearrange(
                             "p j s -> p (j s)"),
                         u_half[:], AF.Exp)

        E.wait("t", "ss", exp_s)
        for jc in range(NJ - 1):
            E.raw("t", "matmul", paux[0:1, 0:S], ones_bf[:], E_sb[:, jc, :],
                  start=(jc == 0), stop=False)
        ssum_t = E.op("t", "matmul", paux[0:1, 0:S], ones_bf[:],
                      E_sb[:, NJ - 1, :], start=(NJ == 1), stop=True)
        E.wait("v", "ts", ssum_t)
        srow_v = E.chain("v", "tensor_copy", s_row[:], paux[0:1, 0:S])
        E.wait("t", "vs", srow_v)
        tr_v = None
        for c in range(SC):
            tr_v = E.op("t", "matmul", paux[:, 256 + c:257 + c],
                        s_row[0:1, c * 128:(c + 1) * 128], ones_f[0:1, 0:1],
                        start=True, stop=True)
        E.wait("v", "ts", tr_v)
        rsv = E.chain("v", "reciprocal", recip_s[:], paux[:, 256:256 + SC])

        QW = 7.0
        dq_state = {"prev_used_v": 0, "prev_wq": [], "prev_stage": None,
                    "ts_guard": None}
        dq_ag = {}

        def mk_dq(name, w_ap, n_rt, cols, wq_d_ap, wT_in_ap, g_in, g_out):
            nblk = cols // 32
            nct = cols // 128
            rows = n_rt * 128
            R = 4 if cols == 128 else 2
            raw_sl = [dq_raw(k, cols) for k in range(R)]
            wq_sl = [dq_wq(k, cols) for k in range(2)]
            st = {"ld": {}, "used_v": {}, "wq_st": {}}

            def issue_ld(j):
                if j >= n_rt:
                    return
                if j - R in st["used_v"]:
                    E.wait("s", "vs", st["used_v"][j - R])
                elif j < R:
                    E.wait("s", "vs", dq_state["prev_used_v"])
                st["ld"][j] = E.dma("sd", raw_sl[j % R],
                                    w_ap[j * 128:(j + 1) * 128, :])

            def unit(rt):
                if rt == 0:
                    for j in range(min(R, n_rt)):
                        issue_ld(j)
                else:
                    issue_ld(rt + R - 1)
                wv = raw_sl[rt % R].rearrange("p (n k) -> p n k", k=32)
                E.wtok("v", st["ld"][rt])
                E.chain("v", "tensor_reduce", wsca[:, 0:nblk][:, :, None], wv,
                        AXX, ALU.max, apply_absolute_value=True)
                E.chain("v", "tensor_scalar", wsca[:, nblk:2 * nblk],
                        wsca[:, 0:nblk], 1.0 / QW, 1e-12, ALU.mult, ALU.max)
                E.chain("v", "reciprocal", wsca[:, 2 * nblk:3 * nblk],
                        wsca[:, nblk:2 * nblk])
                rs_b = wsca[:, 2 * nblk:3 * nblk][:, :, None].broadcast_to(
                    [128, nblk, 32])
                s_b = wsca[:, nblk:2 * nblk][:, :, None].broadcast_to(
                    [128, nblk, 32])
                used = E.chain("v", "tensor_tensor",
                               fscr[:, 0:cols].rearrange("p (n k) -> p n k", k=32),
                               wv, rs_b, ALU.mult)
                st["used_v"][rt] = used
                E.chain("v", "tensor_scalar", fscr[:, 0:cols], fscr[:, 0:cols],
                        MAGIC, MAGIC, ALU.add, ALU.subtract)
                E.chain("v", "tensor_scalar", fscr[:, 0:cols], fscr[:, 0:cols],
                        QW, -QW - 1.0, ALU.min, ALU.max)
                if rt - 2 in st["wq_st"]:
                    E.wtok("v", st["wq_st"][rt - 2])
                elif rt < 2:
                    for tok in dq_state["prev_wq"]:
                        E.wtok("v", tok)
                wq_v = E.chain("v", "tensor_tensor",
                               wq_sl[rt % 2].rearrange("p (n k) -> p n k", k=32),
                               fscr[:, 0:cols].rearrange("p (n k) -> p n k", k=32),
                               s_b, ALU.mult)
                E.wait("sync", "vs", wq_v)
                st["wq_st"][rt] = E.dma(
                    "sync", wq_d_ap[rt * 128:(rt + 1) * 128, :], wq_sl[rt % 2])
                if rt == n_rt - 1:
                    dq_state["prev_used_v"] = used
                    dq_state["prev_wq"] = [st["wq_st"][n_rt - 2],
                                           st["wq_st"][n_rt - 1]]

            def finish():
                for tok in st["wq_st"].values():
                    E.wtok("sync", tok)
                E.wtok("sync", dq_state["prev_stage"])
                if dq_state["ts_guard"] is not None:
                    E.wait("sync", "ts", dq_state["ts_guard"])
                    dq_state["ts_guard"] = None
                wtr = dq_wtr(nct * rows)
                tr_lds = [E.dma("sync", wtr[:, ct * rows:(ct + 1) * rows],
                                wq_d_ap[:, ct * 128:(ct + 1) * 128],
                                transpose=True) for ct in range(nct)]
                for tok in tr_lds:
                    E.wtok("sync", tok)
                if nct == 1:
                    stage = E.dma("sync", wT_in_ap, wtr[:, 0:rows])
                else:
                    stage = E.dma("sync",
                                  wT_in_ap.rearrange("(ct p) d -> p ct d", p=128),
                                  wtr[:, 0:nct * rows].rearrange(
                                      "p (ct r) -> p ct r", ct=nct))
                dq_state["prev_stage"] = stage
                dq_ag[name + "_st"] = stage
                if name != "down":
                    E.wtok("g", stage)
                    dq_ag[name] = E.op("g", "collective_compute", "AllGather",
                                       ALU.bypass, RG8, [g_in], [g_out])

            return [lambda rt=rt: unit(rt) for rt in range(n_rt)] + [finish]

        dq_units = (mk_dq("up", wup_sh, IC, 128, wqup_d, wupT_in,
                          wupT_in, wupT_g)
                    + mk_dq("gate", wgate_sh, IC, 128, wqgate_d, wgateT_in,
                            wgateT_in, wgateT_g)
                    + mk_dq("down", wdown_sh, DC, WDSH, wqdown_d, wdownT_in,
                            wdownT_in, wdownT_g))
        dq_pos = [0]

        def drain_dq(n):
            k = 0
            while dq_pos[0] < len(dq_units) and k < n:
                dq_units[dq_pos[0]]()
                dq_pos[0] += 1
                k += 1

        if phases <= 3:
            dummy = E.dma("sync", y_out.rearrange("b (c p) d -> b c p d", p=128)[0, 0],
                          scr1[:])
            E.wtok("sync", dummy)
            emit_blocks()
            return nc

        if phases <= 4:
            dummy = E.dma("sync", y_out.rearrange("b (c p) d -> b c p d", p=128)[0, 0],
                          scr1[:])
            E.wtok("sync", dummy)
            emit_blocks()
            return nc

        xres_t = x_in.rearrange("b (c p) d -> b c p d", p=128)
        xatt_ld = {}
        batch_mm_done = {}
        att_ev = {}
        ln2_prev_ssq = [prev_ssq]
        ln2_red_prev = [0]
        dq_state["ts_guard"] = E.val("ts")
        dh0_done = {}
        for b in range(B):
            if b >= 1:
                E.wait("sync", "ts", dh0_done[b - 1])
            E.wait("sync", "gs", xln_gath_g[b])
            src = xln_g[b].rearrange("r e p c d -> e p r c d")
            xatt_ld[(b, 0)] = E.dma("sync", xatt4[0][:], src[0])
            if b >= 1:
                E.wait("sync", "ts", batch_mm_done[b - 1])
            xatt_ld[(b, 1)] = E.dma("sync", xatt4[1][:], src[1])
            for dh in range(2):
                buf = xatt[dh]
                k = b * 2 + dh
                for i in range(SC):
                    gi = k * SC + i
                    bank = pbank[2 + gi % 4]
                    if gi >= 4:
                        E.wait("t", "vs", att_ev[gi - 4])
                    E.wtok("t", xatt_ld[(b, dh)])
                    for jc in range(NJ - 1):
                        E.raw("t", "matmul", bank[:], E_sb[:, jc, i * 128:(i + 1) * 128],
                              buf[:, jc, :], start=(jc == 0), stop=False)
                    mm_t = E.op("t", "matmul", bank[:],
                                E_sb[:, NJ - 1, i * 128:(i + 1) * 128],
                                buf[:, NJ - 1, :], start=(NJ == 1), stop=True)
                    slot = gi % 2
                    if gi >= 2:
                        E.wait("s", "vs", att_ev[gi - 2])
                    xr_ld = E.dma("sd", xres[:, slot, :],
                                  xres_t[b, i][:, dh * 512:(dh + 1) * 512])
                    E.wait("v", "ts", mm_t)
                    E.wtok("v", xr_ld)
                    if gi == 0:
                        E.wait("v", "vs", rsv)
                    att_ev[gi] = E.op("v", "scalar_tensor_tensor",
                                      y1[:, b * SC + i, dh * 512:(dh + 1) * 512],
                                      bank[:], recip_s[:, i:i + 1], xres[:, slot, :],
                                      ALU.mult, ALU.add)
                    drain_dq(3)
                if dh == 0:
                    dh0_done[b] = E.val("ts")
            batch_mm_done[b] = E.val("ts")
            for c2 in range(SC):
                t = b * SC + c2
                E.wait("v", "vs", att_ev[(b * 2 + 1) * SC + c2])
                E.wait("s", "vs", att_ev[(b * 2 + 1) * SC + c2])
                E.wait("s", "vs", ln2_red_prev[0])
                _, ln2_prev_ssq[0], x2_v = ln_tile(
                    y1[:, t, :], scr2[:], ln2_prev_ssq[0],
                    mean_ap=ln2_mean[:, t:t + 1], isd_ap=ln2_isd[:, t:t + 1])
                E.wait("v", "ss", x2_v)
                ln2_red_prev[0] = E.chain("v", "tensor_reduce", absmax[:, 1:2],
                                          scr2[:], AXX, ALU.max,
                                          apply_absolute_value=True)
                if t == 0:
                    E.chain("v", "tensor_copy", absmax[:, 0:1], absmax[:, 1:2])
                else:
                    E.chain("v", "tensor_tensor", absmax[:, 0:1],
                            absmax[:, 0:1], absmax[:, 1:2], ALU.max)
        attn_all_v = att_ev[2 * B * SC - 1]
        drain_dq(999)

        if phases <= 5:
            dummy = E.dma("sync", y_out.rearrange("b (c p) d -> b c p d", p=128)[0, 0],
                          scr1[:])
            E.wtok("sync", dummy)
            emit_blocks()
            return nc

        ln2_all_v = E.val("vs")

        def barrier_maxes(cols, bar_in_ap, bar_out_ap):
            n = len(cols)
            E.wait("t", "vs", E.val("vs"))
            E.wtok("t", ld_idf)
            rd_v = 0
            for ii, cc in enumerate(cols):
                if ii > 0:
                    E.wait("t", "vs", rd_v)
                tr = E.op("t", "transpose", paux[0:1, 0:128],
                          absmax[:, cc:cc + 1], id_f[:])
                E.wait("v", "ts", tr)
                rd_v = E.chain("v", "reduce_max", bar_st[:, 8 + ii:9 + ii],
                               paux[0:1, 0:128], AXX)
            E.wait("sync", "vs", rd_v)
            st = E.dma("sync", bar_in_ap[:, 0:n], bar_st[:, 8:8 + n])
            E.wtok("g", st)
            gg = E.op("g", "collective_compute", "AllReduce", ALU.max,
                      RG8, [bar_in_ap[:, 0:n]], [bar_out_ap[:, 0:n]])
            E.wait("sync", "gs", gg)
            return E.dma("sync", bar_st[:, 0:n], bar_out_ap[:, 0:n])

        def bcast(src_col, dst_col, n):
            E.wait("t", "vs", E.val("vs"))
            mmv = E.op("t", "matmul", paux[:, 0:n], ones_f[:],
                       bar_st[:, src_col:src_col + n], start=True, stop=True)
            E.wait("v", "ts", mmv)
            return E.chain("v", "tensor_copy", scal_bc[:, dst_col:dst_col + n],
                           paux[:, 0:n])

        b1_rd = barrier_maxes([0], bars_in[0], bars_out[0])
        E.wtok("g", dq_ag["down_st"])
        dq_ag["down"] = E.op("g", "collective_compute", "AllGather",
                             ALU.bypass, RG8, [wdownT_in], [wdownT_g])
        E.wtok("v", b1_rd)
        E.chain("v", "tensor_scalar", bar_st[:, 2:3], bar_st[:, 0:1],
                1.0 / QMAX, 1e-12, ALU.mult, ALU.max)
        E.chain("v", "reciprocal", bar_st[:, 3:4], bar_st[:, 2:3])
        bc1_v = bcast(2, 0, 2)

        E.wait("v", "vs", bc1_v)
        E.wait("v", "ss", E.val("ss"))
        E.chain("v", "tensor_tensor", sc_xq[:], ln2_isd[:],
                scal_bc[:, 1:2].broadcast_to([128, NT]), ALU.mult)
        E.chain("v", "tensor_tensor", ln2_mean[:], ln2_mean[:], sc_xq[:],
                ALU.mult)
        pre_v = E.chain("v", "tensor_scalar", ln2_mean[:], ln2_mean[:],
                        -1.0, MAGIC, ALU.mult, ALU.add)
        xq_d_t = xq_d.rearrange("(t p) d -> t p d", p=128)
        xq_tile_st = {}
        xq_s = {}
        xq_vv = {}
        E.wait("s", "vs", pre_v)
        for t in range(NT):
            buf = scr1 if t % 2 == 0 else scr2
            if t >= 2:
                E.wait("s", "vs", xq_vv[t - 2])
            xq_s[t] = E.op("s", "activation", buf[:], y1[:, t, :],
                           AF.Identity, scale=sc_xq[:, t:t + 1],
                           bias=ln2_mean[:, t:t + 1])
            E.wait("v", "ss", xq_s[t])
            slot = t % 2
            if t >= 2:
                E.wtok("v", xq_tile_st[t - 2])
            xq_v = E.chain("v", "tensor_scalar", xq_ring[:, slot, :], buf[:],
                           MAGIC, scal_bc[:, 0:1], ALU.subtract, ALU.mult)
            xq_vv[t] = xq_v
            E.wait("sync", "vs", xq_v)
            xq_tile_st[t] = E.dma("sync", xq_d_t[t], xq_ring[:, slot, :])
        for t in range(NT):
            E.wtok("sync", xq_tile_st[t])
        xqT_lds = []
        for dc in range(DC):
            xqT_lds.append(E.dma("sync", xqT[:, dc, :],
                                 xq_d[:, dc * 128:(dc + 1) * 128], transpose=True))

        if phases <= 6:
            dummy = E.dma("sync", y_out.rearrange("b (c p) d -> b c p d", p=128)[0, 0],
                          scr1[:])
            E.wtok("sync", dummy)
            emit_blocks()
            return nc

        up_spT = up_sp.rearrange("(ic p) t -> ic p t", p=128)
        gate_spT = gate_sp.rearrange("(ic p) t -> ic p t", p=128)
        tp_spT = tp_sp.rearrange("(ic p) t -> ic p t", p=128)
        wupT_v = wupT_g.rearrange("r p i -> p r i")
        wgateT_v = wgateT_g.rearrange("r p i -> p r i")
        ug_spill = {}
        wb_mm_done = {}
        first_max = True
        E.wait("sync", "gs", dq_ag["up"])
        E.wait("sync", "gs", dq_ag["gate"])
        ev_s = 0
        ev_v = 0
        last_red_v = 0
        sp_done_for_stage = None
        ev_ring = {}
        _P10N = int(_os.environ.get("P10N", "999"))
        for ic in range(min(IC, _P10N)):
            wb = wbuf[ic % 2]
            if ic >= 2:
                E.wait("sync", "ts", wb_mm_done[ic - 2])
            l1 = E.dma("sync", wb[:, 0:DC, :],
                       wupT_v[:, :, ic * 128:(ic + 1) * 128])
            l2 = E.dma("sync", wb[:, DC:2 * DC, :],
                       wgateT_v[:, :, ic * 128:(ic + 1) * 128])
            E.wtok("t", l1)
            E.wtok("t", l2)
            if ic == 0:
                for tok in xqT_lds:
                    E.wtok("t", tok)
            NQ = T // 512
            for tc in range(NQ):
                gstep = ic * NQ + tc
                bU = pbank[(2 * gstep) % 6]
                bG = pbank[(2 * gstep + 1) % 6]
                if gstep >= 3:
                    E.wait("t", "ss", ev_ring[gstep - 3][0])
                    E.wait("t", "vs", ev_ring[gstep - 3][1])
                mm_t = None
                for dc in range(DC):
                    last = (dc == DC - 1)
                    E.raw("t", "matmul", bU[:], wb[:, dc, :],
                          xqT[:, dc, tc * 512:(tc + 1) * 512],
                          start=(dc == 0), stop=last)
                    if not last:
                        E.raw("t", "matmul", bG[:], wb[:, DC + dc, :],
                              xqT[:, dc, tc * 512:(tc + 1) * 512],
                              start=(dc == 0), stop=False)
                    else:
                        mm_t = E.op("t", "matmul", bG[:],
                                    wb[:, DC + dc, :],
                                    xqT[:, dc, tc * 512:(tc + 1) * 512],
                                    start=(dc == 0), stop=True)
                E.wait("s", "ts", mm_t)
                E.wait("v", "ts", mm_t)
                if tc == 0:
                    if sp_done_for_stage:
                        E.wtok("s", sp_done_for_stage[0])
                        E.wtok("s", sp_done_for_stage[1])
                        E.wtok("v", sp_done_for_stage[0])
                        E.wtok("v", sp_done_for_stage[1])
                    E.wait("s", "vs", last_red_v)
                    E.wait("v", "ss", ev_s)
                sc_c = E.op("s", "activation",
                            ug_stage[0][:, tc * 512:(tc + 1) * 512],
                            bU[:], AF.Copy)
                ve_c = E.op("v", "tensor_copy",
                            ug_stage[1][:, tc * 512:(tc + 1) * 512],
                            bG[:])
                ev_ring[gstep] = (sc_c, ve_c)
                ev_s = sc_c
                ev_v = ve_c
            E.wait("v", "ss", ev_s)
            E.chain("v", "tensor_reduce", wsca[:, ic:ic + 1], ug_stage[0][:],
                    AXX, ALU.max, apply_absolute_value=True)
            last_red_v = E.chain("v", "tensor_reduce", wsca[:, IC + ic:IC + ic + 1],
                                 ug_stage[1][:], AXX, ALU.max,
                                 apply_absolute_value=True)
            wb_mm_done[ic] = E.val("ts")
            if _os.environ.get("P10SPILL", "1") == "0":
                ug_spill[ic] = None
                continue
            E.wait("s", "ss", ev_s)
            s1d = E.dma("sd", up_spT[ic], ug_stage[0][:])
            s2d = E.dma("sd", gate_spT[ic], ug_stage[1][:])
            ug_spill[ic] = (s1d, s2d)
            sp_done_for_stage = (s1d, s2d)

        if _os.environ.get("P10B", "1") == "0":
            dummy = E.dma("sync", y_out.rearrange("b (c p) d -> b c p d", p=128)[0, 0],
                          scr1[:])
            E.wtok("sync", dummy)
            emit_blocks()
            return nc
        E.chain("v", "tensor_reduce", absmax[:, 2:3], wsca[:, 0:IC],
                AXX, ALU.max)
        E.chain("v", "tensor_reduce", absmax[:, 3:4], wsca[:, IC:2 * IC],
                AXX, ALU.max)
        tp_st = {}
        p1_ss = {}
        k3_vv = {}
        sig_ss = {}
        stt_vv = {}
        g_lds = {}

        def f5_ld(j):
            if j >= IC:
                return
            E.wtok("sync", ug_spill[j][1])
            if j >= 3:
                E.wait("sync", "ss", p1_ss[j - 3])
            g_lds[j] = E.dma("sync", g_rd[j % 3][:], gate_spT[j])

        for _ic in range(max(0, IC - 4), IC):
            E.wtok("sync", ug_spill[_ic][0])
            E.wtok("sync", ug_spill[_ic][1])
        E.wait("sync", "vs", E.val("vs"))
        for j in range(3):
            f5_ld(j)
        b2_rd = barrier_maxes([2, 3], bars_in[1], bars_out[1])
        E.wtok("v", b2_rd)
        E.chain("v", "tensor_scalar", bar_st[:, 2:3], bar_st[:, 0:1],
                1.0 / QMAX, 1e-12, ALU.mult, ALU.max)
        E.chain("v", "reciprocal", bar_st[:, 3:4], bar_st[:, 2:3])
        E.chain("v", "tensor_scalar", bar_st[:, 4:5], bar_st[:, 1:2],
                1.0 / QMAX, 1e-12, ALU.mult, ALU.max)
        E.chain("v", "reciprocal", bar_st[:, 5:6], bar_st[:, 4:5])
        E.chain("v", "tensor_scalar_mul", bar_st[:, 6:7], bar_st[:, 4:5],
                -MAGIC)
        bc2_v = bcast(2, 2, 5)

        if phases <= 7:
            dummy = E.dma("sync", y_out.rearrange("b (c p) d -> b c p d", p=128)[0, 0],
                          scr1[:])
            E.wtok("sync", dummy)
            emit_blocks()
            return nc

        E.wait("v", "vs", bc2_v)
        E.wait("s", "vs", bc2_v)
        for it in range(IC + 2):
            if it < IC:
                ic = it
                rb = g_rd[ic % 3]
                gq = gq_sb[ic % 3]
                E.wtok("s", g_lds[ic])
                if ic >= 3:
                    E.wait("s", "vs", stt_vv[ic - 3])
                p1_ss[ic] = E.op("s", "activation", gq[:], rb[:], AF.Copy,
                                 scale=scal_bc[:, 5:6], bias=MAGIC)
                f5_ld(ic + 3)
            if 1 <= it < IC + 1:
                ic = it - 1
                gq = gq_sb[ic % 3]
                sg = sg_sb[ic % 3]
                if ic >= 3:
                    E.wait("s", "vs", stt_vv[ic - 3])
                E.op("s", "activation", sg[:], gq[:], AF.Sigmoid,
                     scale=scal_bc[:, 4:5], bias=scal_bc[:, 6:7])
                sig_ss[ic] = E.op("s", "activation", sg[:], sg[:], AF.Copy,
                                  scale=65536.0, bias=MAGIC)
            if it >= 2:
                ic = it - 2
                gq = gq_sb[ic % 3]
                sg = sg_sb[ic % 3]
                tb = tp_sb[ic % 2]
                E.wait("v", "ss", sig_ss[ic])
                E.chain("v", "tensor_scalar", sg[:], sg[:],
                        MAGIC, 65535.0, ALU.subtract, ALU.min)
                if ic >= 2:
                    E.wtok("v", tp_st[ic - 2])
                stt_vv[ic] = E.chain("v", "scalar_tensor_tensor", tb[:],
                                     gq[:], MAGIC, sg[:],
                                     ALU.subtract, ALU.mult)
                E.chain("v", "tensor_reduce", wsca[:, 2 * IC + ic:2 * IC + ic + 1],
                        tb[:], AXX, ALU.max, apply_absolute_value=True)
                E.wait("sync", "vs", stt_vv[ic])
                tp_st[ic] = E.dma("sync", tp_spT[ic], tb[:])

        E.chain("v", "tensor_reduce", absmax[:, 4:5], wsca[:, 2 * IC:3 * IC],
                AXX, ALU.max)
        t2_st = {}
        p1b_ss = {}
        p2b_ss = {}
        k4_vv = {}
        q2b_gg = {}
        stt2_vv = {}
        tp_lds = {}
        up_lds = {}

        def f7_ld(j):
            if j >= IC:
                return
            E.wtok("sync", tp_st[j])
            E.wtok("sync", ug_spill[j][0])
            if j == 0:
                E.wtok("sync", dq_ag["down_st"])
            if j >= 3:
                E.wait("sync", "ss", p1b_ss[j - 3])
            else:
                E.wait("sync", "ss", p1_ss[IC - 3 + j])
            tp_lds[j] = E.dma("sync", g_rd[j % 3][:], tp_spT[j])
            if j >= 3:
                E.wait("sync", "ss", p2b_ss[j - 3])
            up_lds[j] = E.dma("sync", u_rd[j % 3][:], up_spT[j])

        for j in range(3):
            f7_ld(j)
        b3_rd = barrier_maxes([4], bars_in[2], bars_out[2])
        E.wtok("v", b3_rd)
        E.chain("v", "tensor_scalar", bar_st[:, 7:8], bar_st[:, 0:1],
                bar_st[0:1, 4:5], 1.0 / 65536.0 / QMAX, ALU.mult, ALU.mult)
        E.chain("v", "tensor_scalar_max", bar_st[:, 7:8], bar_st[:, 7:8],
                1e-12)
        E.chain("v", "reciprocal", bar_st[:, 3:4], bar_st[:, 7:8])
        E.chain("v", "tensor_scalar", bar_st[:, 3:4], bar_st[:, 3:4],
                bar_st[0:1, 4:5], 1.0 / 65536.0, ALU.mult, ALU.mult)
        bc3_v = bcast(3, 7, 1)

        if phases <= 8:
            dummy = E.dma("sync", y_out.rearrange("b (c p) d -> b c p d", p=128)[0, 0],
                          scr1[:])
            E.wtok("sync", dummy)
            emit_blocks()
            return nc

        E.wait("v", "vs", bc3_v)
        E.wait("s", "vs", bc3_v)
        for it in range(IC + 1):
            if it < IC:
                ic = it
                rb = g_rd[ic % 3]
                ub = u_rd[ic % 3]
                gq = gq_sb[ic % 3]
                sg = sg_sb[ic % 3]
                E.wtok("s", tp_lds[ic])
                if ic >= 3:
                    E.wait("s", "vs", stt2_vv[ic - 3])
                p1b_ss[ic] = E.op("s", "activation", gq[:], rb[:], AF.Copy,
                                  scale=scal_bc[:, 7:8], bias=MAGIC)
                E.wtok("s", up_lds[ic])
                if ic >= 3:
                    E.wait("s", "vs", stt2_vv[ic - 3])
                p2b_ss[ic] = E.op("s", "activation", sg[:], ub[:], AF.Copy,
                                  scale=scal_bc[:, 3:4], bias=MAGIC)
                f7_ld(ic + 3)
                E.wait("v", "ss", p1b_ss[ic])
                k4_vv[ic] = E.chain("v", "tensor_scalar", gq[:], gq[:],
                                    MAGIC, None, ALU.subtract)
            if it >= 1:
                ic = it - 1
                gq = gq_sb[ic % 3]
                sg = sg_sb[ic % 3]
                tb = tp_sb[ic % 2]
                E.wait("v", "ss", p2b_ss[ic])
                if ic >= 2:
                    E.wtok("v", t2_st[ic - 2])
                stt2_vv[ic] = E.chain("v", "scalar_tensor_tensor", tb[:],
                                      sg[:], MAGIC, gq[:],
                                      ALU.subtract, ALU.mult)
                E.chain("v", "tensor_reduce",
                        wsca[:, 3 * IC + ic:3 * IC + ic + 1],
                        tb[:], AXX, ALU.max, apply_absolute_value=True)
                E.wait("sync", "vs", stt2_vv[ic])
                t2_st[ic] = E.dma("sync", gate_spT[ic], tb[:])

        E.chain("v", "tensor_reduce", absmax[:, 6:7], wsca[:, 3 * IC:4 * IC],
                AXX, ALU.max)
        t2_view = gate_sp.rearrange("(ic p) t -> p ic t", p=128)
        t2_lds = {}
        o_v = {}

        def p16_ld(j):
            if j >= NT:
                return
            if j == 0:
                for _tok in t2_st.values():
                    E.wtok("sync", _tok)
            if j >= 2:
                E.wait("sync", "vs", o_v[j - 2])
            t2_lds[j] = E.dma("sync", t2_rd[j % 2][:],
                              t2_view[:, :, j * 128:(j + 1) * 128])

        for j in range(2):
            p16_ld(j)
        b4_rd = barrier_maxes([6], bars_in[3], bars_out[3])
        E.wtok("v", b4_rd)
        E.chain("v", "tensor_scalar", bar_st[:, 6:7], bar_st[:, 7:8],
                bar_st[0:1, 2:3], None, ALU.mult)
        E.chain("v", "tensor_scalar", bar_st[:, 4:5], bar_st[:, 0:1],
                bar_st[0:1, 6:7], 1.0 / QMAX, ALU.mult, ALU.mult)
        E.chain("v", "tensor_scalar_max", bar_st[:, 4:5], bar_st[:, 4:5],
                1e-12)
        E.chain("v", "reciprocal", bar_st[:, 5:6], bar_st[:, 4:5])
        E.chain("v", "tensor_scalar", bar_st[:, 5:6], bar_st[:, 5:6],
                bar_st[0:1, 6:7], None, ALU.mult)
        bc4_v = bcast(4, 8, 2)

        E.wait("sync", "gs", dq_ag["down"])
        wdT_ld = E.dma("sync", wdT_sb[:],
                       wdownT_g.rearrange("r w d -> (r w) d").rearrange(
                           "(ic p) d -> p ic d", p=128))

        if phases <= 9:
            dummy = E.dma("sync", y_out.rearrange("b (c p) d -> b c p d", p=128)[0, 0],
                          scr1[:])
            E.wtok("sync", dummy)
            emit_blocks()
            return nc

        dn_done = {}
        fin_done = {}
        out_st = {}
        E.wait("v", "vs", bc4_v)
        E.wait("s", "vs", bc4_v)
        for tc in range(NT + 1):
            if tc < NT:
                rb = t2_rd[tc % 2]
                ob = o_buf[tc % 2]
                rv = rb[:].rearrange("p a q -> p (a q)")
                ov = ob[:].rearrange("p a q -> p (a q)")
                E.wtok("s", t2_lds[tc])
                if tc >= 2:
                    E.wait("s", "vs", o_v[tc - 2])
                p1_s = E.op("s", "activation", rv, rv, AF.Copy,
                            scale=scal_bc[:, 9:10], bias=MAGIC)
                E.wait("v", "ss", p1_s)
                o_v[tc] = E.chain("v", "tensor_scalar", ov, rv,
                                  MAGIC, scal_bc[:, 8:9], ALU.subtract, ALU.mult)
                p16_ld(tc + 2)
                E.wait("t", "vs", o_v[tc])
                E.wtok("t", wdT_ld)
                bk = 2 * (tc % 2)
                if tc >= 2:
                    E.wait("t", "vs", fin_done[tc - 2])
                dn_t = None
                for d2c in range(2):
                    for icx in range(IC):
                        last = (d2c == 1) and (icx == IC - 1)
                        if not last:
                            E.raw("t", "matmul", pbank[bk + d2c][:], ob[:, icx, :],
                                  wdT_sb[:, icx, d2c * 512:(d2c + 1) * 512],
                                  start=(icx == 0), stop=(icx == IC - 1))
                        else:
                            dn_t = E.op("t", "matmul", pbank[bk + d2c][:],
                                        ob[:, icx, :],
                                        wdT_sb[:, icx, d2c * 512:(d2c + 1) * 512],
                                        start=(icx == 0), stop=True)
                dn_done[tc] = dn_t
            if tc >= 1:
                tp = tc - 1
                bkp = 2 * (tp % 2)
                slot = tp % 2
                E.wait("v", "ts", dn_done[tp])
                if tp >= 2:
                    E.wtok("v", out_st[tp - 2])
                E.chain("v", "tensor_add", out_sb[:, slot, 0:512], pbank[bkp][:],
                        y1[:, tp, 0:512])
                fin_done[tp] = E.chain("v", "tensor_add",
                                       out_sb[:, slot, 512:1024],
                                       pbank[bkp + 1][:], y1[:, tp, 512:1024])
                b, c = tp // SC, tp % SC
                E.wait("sync", "vs", fin_done[tp])
                out_st[tp] = E.dma("sync",
                                   y_out.rearrange("b (c p) d -> b c p d", p=128)[b, c],
                                   out_sb[:, slot, :])

        E.wtok("sync", out_st[NT - 1])

        emit_blocks()
        return nc
    return nc



def kernel(L0_attn_x, L0_ffn_w_gate, L0_ffn_w_up, L0_ffn_w_down):
    x = np.asarray(L0_attn_x, dtype=np.float32)
    wg = np.asarray(L0_ffn_w_gate, dtype=np.float32)
    wu = np.asarray(L0_ffn_w_up, dtype=np.float32)
    wd = np.asarray(L0_ffn_w_down, dtype=np.float32)
    B, L, D = x.shape
    I = wu.shape[0]
    n = 8
    S = L // n
    nc = build_nc(B=B, L=L, D=D, H=16, I=I, n_cores=n)
    ident_bf = np.eye(128, dtype=ml_dtypes.bfloat16)
    ident_f = np.eye(128, dtype=np.float32)
    in_maps = []
    for c in range(n):
        in_maps.append({
            "x": np.ascontiguousarray(x[:, S * c:S * (c + 1), :]),
            "wup_sh": np.ascontiguousarray(wu[:, (D // n) * c:(D // n) * (c + 1)]),
            "wgate_sh": np.ascontiguousarray(wg[:, (D // n) * c:(D // n) * (c + 1)]),
            "wdown_sh": np.ascontiguousarray(wd[:, (I // n) * c:(I // n) * (c + 1)]),
            "ident_bf": ident_bf,
            "ident_f": ident_f,
        })
    res = run_bass_kernel_spmd(nc, in_maps, core_ids=list(range(n)))
    return np.concatenate([res.results[c]["y"] for c in range(n)], axis=1)


# version 39 (final)
# speedup vs baseline: 1.0632x; 1.0055x over previous
import contextlib
import os as _os
import numpy as np
import ml_dtypes
import concourse.bass as bass
import concourse.mybir as mybir
from concourse.bass_utils import run_bass_kernel_spmd

F32 = mybir.dt.float32
BF16 = mybir.dt.bfloat16
AF = mybir.ActivationFunctionType
ALU = mybir.AluOpType
AXX = mybir.AxisListType.X
RG8 = [[0, 1, 2, 3, 4, 5, 6, 7]]
MAGIC = 12582912.0
QMAX = 32767.0
EPS_LN = 1e-5
PI = float(np.pi)


class Sched:

    def __init__(self, nc):
        self.nc = nc
        self.ops = {k: [] for k in ("sync", "v", "s", "t", "g")}
        self.cnt = {k: 0 for k in ("vs", "ss", "ts", "gs")}
        self.sem = {}
        self.esem = {"v": "vs", "s": "ss", "t": "ts", "g": "gs"}

    def make_sems(self, stack):
        self.dma_ring = {"sync": [f"ld{i}" for i in range(6)],
                         "sd": [f"sd{i}" for i in range(4)]}
        self.dma_idx = {"sync": 0, "sd": 0}
        for q in self.dma_ring:
            for name in self.dma_ring[q]:
                self.cnt[name] = 0
        for name in list(self.cnt):
            self.sem[name] = stack.enter_context(self.nc.semaphore(name))

    def dma(self, queue, out, in_, **kw):
        stream = "sync" if queue == "sync" else "s"
        ring = self.dma_ring[queue]
        semname = ring[self.dma_idx[queue] % len(ring)]
        self.dma_idx[queue] += 1
        prev = self.cnt[semname]
        if prev:
            self.wait(stream, semname, prev)
        self.cnt[semname] = prev + 16
        tgt = self.cnt[semname]
        sem = self.sem[semname]

        def f(eng, out=out, in_=in_, kw=kw, sem=sem):
            eng.dma_start(out=out, in_=in_, **kw).then_inc(sem, 16)
        self.ops[stream].append(f)
        return (semname, tgt)

    def wtok(self, engine, token):
        if token:
            self.wait(engine, token[0], token[1])

    def op(self, engine, fname, *args, **kw):
        semname = self.esem[engine]
        prev = self.cnt[semname]
        if prev:
            self.wait(engine, semname, prev)
        self.cnt[semname] = prev + 1
        tgt = self.cnt[semname]
        sem = self.sem[semname]

        def f(eng, fname=fname, args=args, kw=kw, sem=sem):
            getattr(eng, fname)(*args, **kw).then_inc(sem, 1)
        self.ops[engine].append(f)
        return tgt

    def raw(self, engine, fname, *args, **kw):
        def f(eng, fname=fname, args=args, kw=kw):
            getattr(eng, fname)(*args, **kw)
        self.ops[engine].append(f)

    def fence(self, engine):
        return self.op(engine, "nop")

    def wait(self, engine, semname, value):
        if value <= 0:
            return
        stream = engine if engine in self.ops else "s"
        sem = self.sem[semname]

        def f(eng, sem=sem, value=value):
            eng.wait_ge(sem, value)
        self.ops[stream].append(f)

    def chain(self, engine, fname, *args, **kw):
        v = self.op(engine, fname, *args, **kw)
        self.wait(engine, self.esem[engine], v)
        return v

    def val(self, semname):
        return self.cnt[semname]


def build_nc(B=4, L=4096, D=1024, H=16, I=3072, n_cores=8, phases=99):
    DK = D // H
    S = L // n_cores
    T = B * S
    NT = T // 128
    SC = S // 128
    NJ = L // 128
    DC = D // 128
    IC = I // 128
    DSH = D // n_cores
    WDSH = I // n_cores
    assert S % 128 == 0 and DSH == 128 and D == 1024, "layout assumes D=1024"
    assert WDSH % 128 == 0, "w_down shard must be whole 128-row tiles"

    nc = bass.Bass("TRN2", target_bir_lowering=False, debug=False)

    x_in = nc.dram_tensor("x", [B, S, D], F32, kind="ExternalInput").ap()
    wup_sh = nc.dram_tensor("wup_sh", [I, DSH], F32, kind="ExternalInput").ap()
    wgate_sh = nc.dram_tensor("wgate_sh", [I, DSH], F32, kind="ExternalInput").ap()
    wdown_sh = nc.dram_tensor("wdown_sh", [D, WDSH], F32, kind="ExternalInput").ap()
    ident_bf = nc.dram_tensor("ident_bf", [128, 128], BF16, kind="ExternalInput").ap()
    ident_f = nc.dram_tensor("ident_f", [128, 128], F32, kind="ExternalInput").ap()
    y_out = nc.dram_tensor("y", [B, S, D], F32, kind="ExternalOutput").ap()

    dt_i = lambda n, sh, dt: nc.dram_tensor(n, sh, dt, kind="Internal").ap()
    mT_in = dt_i("mT_in", [DK, S], BF16)
    mT_g = dt_i("mT_g", [n_cores, DK, S], BF16)
    xln_in = [dt_i(f"xln_in{b}", [2, 128, S // 128, D // 2], BF16)
              for b in range(B)]
    xln_g = [dt_i(f"xln_g{b}", [n_cores, 2, 128, S // 128, D // 2], BF16)
             for b in range(B)]
    wupT_in = dt_i("wupT_in", [DSH, I], BF16)
    wupT_g = dt_i("wupT_g", [n_cores, DSH, I], BF16)
    wgateT_in = dt_i("wgateT_in", [DSH, I], BF16)
    wgateT_g = dt_i("wgateT_g", [n_cores, DSH, I], BF16)
    wdownT_in = dt_i("wdownT_in", [WDSH, D], BF16)
    wdownT_g = dt_i("wdownT_g", [n_cores, WDSH, D], BF16)
    s_rt = dt_i("s_rt", [1, S], F32)
    xq_d = dt_i("xq_d", [T, D], BF16)
    wqup_d = dt_i("wqup_d", [I, DSH], BF16)
    wqgate_d = dt_i("wqgate_d", [I, DSH], BF16)
    wqdown_d = dt_i("wqdown_d", [D, WDSH], BF16)
    up_sp = dt_i("up_sp", [I, T], F32)
    gate_sp = dt_i("gate_sp", [I, T], F32)
    tp_sp = dt_i("tp_sp", [I, T], F32)
    bars_in = [dt_i(f"bar{i}_in", [1, 2], F32) for i in range(4)]
    bars_out = [dt_i(f"bar{i}_out", [1, 2], F32) for i in range(4)]

    stack = contextlib.ExitStack()
    with stack:
        sb = lambda name, shape, dt: stack.enter_context(nc.sbuf_tensor(name, shape, dt)).ap() \
            if False else stack.enter_context(nc.sbuf_tensor(name, shape, dt))
        psal = lambda name, shape, dt: stack.enter_context(nc.psum_tensor(name, shape, dt))

        I8 = mybir.dt.int8
        ARENA_BYTES = max(131072 + IC * D * 2,
                          163840 + 8 * T,
                          174080 + SC * DK * 2,
                          65536 + NT * D * 4,
                          178688)
        arena = sb("arena", [128, ARENA_BYTES], I8)

        def av(off, n_elem, dt, shape=None):
            esz = 2 if dt == BF16 else 4
            v = arena[:, off:off + n_elem * esz].bitcast(dt)
            if shape is not None:
                v = v.rearrange("p (" + " ".join(f"a{i}" for i in range(len(shape)))
                                + ") -> p " + " ".join(f"a{i}" for i in range(len(shape))),
                                **{f"a{i}": s for i, s in enumerate(shape)})
            return v

        x_ring = av(0, 2 * D, F32, (2, D))
        xln_sb = av(8192, NT * D, BF16, (NT, D))
        dq_raw = lambda k, cols: av(167936 + k * cols * 4, cols, F32)
        dq_wq = lambda k, cols: av(171008 + k * cols * 2, cols, BF16)
        dq_wtr = lambda n: av(172544, n, BF16)
        u_half = av(65536, (NJ // 2) * S, F32)
        E_sb = av(131072, NJ * S, BF16, (NJ, S))
        mT_sb = av(163840, L, BF16)[0:DK, :]
        mT_stage = av(172032, S, BF16)[0:DK, :]
        mT_my = av(173056, S, BF16)[0:DK, :]
        m_bf = av(174080, SC * DK, BF16, (SC, DK))
        xatt = [av(0, NJ * 512, BF16, (NJ, 512)),
                av(32768, NJ * 512, BF16, (NJ, 512))]
        xatt4 = [av(0, NJ * 512, BF16, (n_cores, NJ // n_cores, 512)),
                 av(32768, NJ * 512, BF16, (n_cores, NJ // n_cores, 512))]
        xres = av(163840, 2 * 512, F32, (2, 512))
        y1 = av(65536, NT * D, F32, (NT, D))
        xqT = av(0, DC * T, BF16, (DC, T))
        xq_ring = av(32768, 2 * D, BF16, (2, D))
        wbuf = [av(36864, 2 * DC * 128, BF16, (2 * DC, 128)),
                av(36864 + DC * 512, 2 * DC * 128, BF16, (2 * DC, 128))]
        ug_stage = [av(131072, T, F32), av(131072 + T * 4, T, F32)]
        g_rd = [av(131072, T, F32), av(131072 + T * 4, T, F32),
                av(131072 + 2 * T * 4, T, F32)]
        u_rd = [av(155648, T, F32), av(155648 + T * 4, T, F32),
                av(155648 + 2 * T * 4, T, F32)]
        gq_sb = [av(0, T, F32), av(T * 4, T, F32), av(2 * T * 4, T, F32)]
        sg_sb = [av(24576, T, F32), av(24576 + T * 4, T, F32),
                 av(24576 + 2 * T * 4, T, F32)]
        tp_sb = [av(49152, T, F32), av(49152 + T * 4, T, F32)]
        wdT_sb = av(131072, IC * D, BF16, (IC, D))
        t2_rd = [av(0, IC * 128, F32, (IC, 128)),
                 av(IC * 512, IC * 128, F32, (IC, 128))]
        o_buf = [av(2 * IC * 512, IC * 128, BF16, (IC, 128)),
                 av(2 * IC * 512 + IC * 256, IC * 128, BF16, (IC, 128))]
        out_sb = av(36864, 2 * D, F32, (2, D))

        scr1 = sb("scr1", [128, D], F32)
        scr2 = sb("scr2", [128, D], F32)
        fscr = sb("fscr", [128, 1024], F32)
        stat = sb("stat", [128, 8], F32)
        m_acc = sb("m_acc", [128, SC, DK], F32)
        ones_bf = sb("ones_bf", [128, 1], BF16)
        ones_f = sb("ones_f", [1, 128], F32)
        id_bf = sb("id_bf", [128, 128], BF16)
        id_f = sb("id_f", [128, 128], F32)
        s_row = sb("s_row", [1, S], F32)
        recip_s = sb("recip_s", [128, SC], F32)
        absmax = sb("absmax", [128, 8], F32)
        bar_st = sb("bar_st", [1, 16], F32)
        scal_bc = sb("scal_bc", [128, 12], F32)
        wsca = sb("wsca", [128, 3 * (IC * 128) // 32], F32)
        ln2_mean = sb("ln2_mean", [128, NT], F32)
        ln2_isd = sb("ln2_isd", [128, NT], F32)
        sc_xq = sb("sc_xq", [128, NT], F32)

        pbank = [psal(f"pb{i}", [128, 512], F32) for i in range(6)]
        pbf = psal("pbf", [128, 1024], BF16)
        paux = psal("paux", [128, 512], F32)

        E = Sched(nc)
        E.make_sems(stack)

        def emit_blocks():
            with nc.Block() as block:
                @block.sync
                def _(sync):
                    for f in E.ops["sync"]:
                        f(sync)

                @block.scalar
                def _(scalar):
                    for f in E.ops["s"]:
                        f(scalar)

                @block.vector
                def _(vector):
                    for f in E.ops["v"]:
                        f(vector)

                @block.tensor
                def _(tensor):
                    for f in E.ops["t"]:
                        f(tensor)

                @block.gpsimd
                def _(gpsimd):
                    for f in E.ops["g"]:
                        f(gpsimd)


        E.op("v", "memset", ones_bf[:], 1.0)
        E.op("v", "memset", ones_f[:], 1.0)
        const_v = E.val("vs")
        ld_idf = E.dma("sync", id_f[:], ident_f)
        ld_idb = E.dma("sync", id_bf[:], ident_bf)

        def ln_tile(src_ap, dst_ap, prev_ssq, mean_ap=None, isd_ap=None):
            mean_ap = mean_ap if mean_ap is not None else stat[:, 1:2]
            isd_ap = isd_ap if isd_ap is not None else stat[:, 5:6]
            E.wait("s", "ss", prev_ssq)
            sum_s = E.op("s", "activation", scr2[:], src_ap, AF.Copy,
                         accum_out=stat[:, 0:1])
            E.wait("v", "ss", sum_s)
            E.chain("v", "tensor_scalar_mul", mean_ap, stat[:, 0:1], 1.0 / D)
            xc_s = E.chain("v", "tensor_scalar", scr1[:], src_ap,
                           mean_ap, None, ALU.subtract)
            E.wait("s", "vs", xc_s)
            ssq_s = E.op("s", "activation", scr2[:], scr1[:], AF.Square,
                         accum_out=stat[:, 2:3])
            E.wait("v", "ss", ssq_s)
            E.chain("v", "tensor_scalar", stat[:, 3:4], stat[:, 2:3], 1.0 / D,
                    EPS_LN, ALU.mult, ALU.add)
            E.wait("s", "vs", E.val("vs"))
            sq_s = E.op("s", "activation", stat[:, 4:5], stat[:, 3:4], AF.Sqrt)
            E.wait("v", "ss", sq_s)
            rc_v = E.chain("v", "reciprocal", isd_ap, stat[:, 4:5])
            E.wait("s", "vs", rc_v)
            out_s = E.op("s", "activation", dst_ap, scr1[:], AF.Copy,
                         scale=isd_ap)
            return xc_s, out_s, out_s

        x_t = x_in.rearrange("b (c p) d -> b c p d", p=128)
        ring_used = [0, 0]
        xln_stage_ld = []
        prev_ssq = 0
        fold_g_done = 0
        ln1_v = 0
        for t in range(NT):
            b, c = t // SC, t % SC
            slot = t % 2
            E.wait("sync", "vs", ring_used[slot])
            ld_v = E.dma("sync", x_ring[:, slot, :], x_t[b, c])
            E.wtok("v", ld_v)
            E.wtok("s", ld_v)
            xc_v, prev_ssq, xln_v = ln_tile(x_ring[:, slot, :], xln_sb[:, t, :], prev_ssq)
            ring_used[slot] = xc_v
            ln1_v = xln_v
            xv = xln_sb[:, t, :].rearrange("p (h k) -> p k h", k=DK)
            E.wait("v", "ss", xln_v)
            fold_g_done = E.chain("v", "reduce_sum",
                                  fscr[:, t * DK:(t + 1) * DK][:, :, None],
                                  xv, AXX)
            if c == SC - 1:
                E.wait("sync", "ss", xln_v)
                in_ld0 = E.dma("sync", xln_in[b][0],
                               xln_sb[:, b * SC:(b + 1) * SC, 0:512])
                in_ld1 = E.dma("sync", xln_in[b][1],
                               xln_sb[:, b * SC:(b + 1) * SC, 512:1024])
                xln_stage_ld.append((in_ld0, in_ld1))
        ln1_all_v = ln1_v

        E.wait("v", "vs", fold_g_done)
        fm = fscr[:, 0:NT * DK].rearrange("p (b z) -> p b z", b=B)
        E.chain("v", "tensor_tensor", fm[:, 0, :], fm[:, 0, :], fm[:, 2, :],
                ALU.add)
        E.chain("v", "tensor_tensor", fm[:, 1, :], fm[:, 1, :], fm[:, 3, :],
                ALU.add)
        E.chain("v", "tensor_tensor", m_acc[:].rearrange("p c k -> p (c k)"),
                fm[:, 0, :], fm[:, 1, :], ALU.add)
        mbf_v = E.chain("v", "tensor_scalar", m_bf[:], m_acc[:],
                        1.0 / (B * H), None, ALU.mult)
        E.wait("t", "vs", mbf_v)
        E.wtok("t", ld_idb)
        mT_cp_v = 0
        for c in range(SC):
            if c > 0:
                E.wait("t", "vs", mT_cp_v)
            tr_t = E.op("t", "transpose", pbf[0:DK, 0:128], m_bf[:, c, :], id_bf[:])
            E.wait("v", "ts", tr_t)
            mT_cp_v = E.op("v", "tensor_copy", mT_stage[:, c * 128:(c + 1) * 128],
                           pbf[0:DK, 0:128])
        E.wait("sync", "vs", mT_cp_v)
        mT_st = E.dma("sync", mT_in, mT_stage[:])

        if phases <= 2:
            dummy = E.dma("sync", y_out.rearrange("b (c p) d -> b c p d", p=128)[0, 0],
                          scr1[:])
            E.wtok("sync", dummy)
            emit_blocks()
            return nc

        xln_gath_g = []
        mT_gath_g = None
        for b in range(B):
            E.wtok("g", xln_stage_ld[b][0])
            E.wtok("g", xln_stage_ld[b][1])
            gg = E.op("g", "collective_compute", "AllGather", ALU.bypass,
                      RG8, [xln_in[b]], [xln_g[b]])
            xln_gath_g.append(gg)
            if b == 0:
                E.wtok("g", mT_st)
                mT_gath_g = E.op("g", "collective_compute", "AllGather",
                                 ALU.bypass, RG8, [mT_in], [mT_g])
        E.wait("sync", "gs", mT_gath_g)
        mT_rd = E.dma("sync", mT_sb[:].rearrange("k (r s) -> k r s", r=n_cores),
                      mT_g.rearrange("r k s -> k r s"))
        my_rd = E.dma("sync", mT_my[:], mT_in)

        E.wtok("t", mT_rd)
        E.wtok("t", my_rd)
        NJH = NJ // 2
        clip_v = {}
        exp_s = 0
        for half in range(2):
            for jj in range(NJH):
                jc = half * NJH + jj
                bank = pbank[jc % 2]
                if jc >= 2:
                    E.wait("t", "vs", clip_v[jc - 2])
                mm_t = E.op("t", "matmul", bank[:, 0:S],
                            mT_sb[:, jc * 128:(jc + 1) * 128],
                            mT_my[:], start=True, stop=True)
                E.wait("v", "ts", mm_t)
                if jj == 0:
                    E.wait("v", "ss", exp_s)
                clip_v[jc] = E.op("v", "tensor_scalar",
                                  u_half[:, jj * S:(jj + 1) * S],
                                  bank[:, 0:S], PI, -PI, ALU.min, ALU.max)
            E.wait("s", "vs", clip_v[half * NJH + NJH - 1])
            sin_s = E.op("s", "activation", u_half[:], u_half[:], AF.Sin, scale=0.5)
            E.wait("s", "ss", sin_s)
            sq2_s = E.op("s", "activation", u_half[:], u_half[:], AF.Square)
            E.wait("s", "ss", sq2_s)
            exp_s = E.op("s", "activation",
                         E_sb[:, half * NJH:(half + 1) * NJH, :].rearrange(
                             "p j s -> p (j s)"),
                         u_half[:], AF.Exp)

        E.wait("t", "ss", exp_s)
        for jc in range(NJ - 1):
            E.raw("t", "matmul", paux[0:1, 0:S], ones_bf[:], E_sb[:, jc, :],
                  start=(jc == 0), stop=False)
        ssum_t = E.op("t", "matmul", paux[0:1, 0:S], ones_bf[:],
                      E_sb[:, NJ - 1, :], start=(NJ == 1), stop=True)
        E.wait("v", "ts", ssum_t)
        srow_v = E.chain("v", "tensor_copy", s_row[:], paux[0:1, 0:S])
        E.wait("t", "vs", srow_v)
        tr_v = None
        for c in range(SC):
            tr_v = E.op("t", "matmul", paux[:, 256 + c:257 + c],
                        s_row[0:1, c * 128:(c + 1) * 128], ones_f[0:1, 0:1],
                        start=True, stop=True)
        E.wait("v", "ts", tr_v)
        rsv = E.chain("v", "reciprocal", recip_s[:], paux[:, 256:256 + SC])

        QW = 7.0
        dq_state = {"prev_used_v": 0, "prev_wq": [], "prev_stage": None,
                    "ts_guard": None}
        dq_ag = {}

        def mk_dq(name, w_ap, n_rt, cols, wq_d_ap, wT_in_ap, g_in, g_out):
            nblk = cols // 32
            nct = cols // 128
            rows = n_rt * 128
            R = 4 if cols == 128 else 2
            raw_sl = [dq_raw(k, cols) for k in range(R)]
            wq_sl = [dq_wq(k, cols) for k in range(2)]
            st = {"ld": {}, "used_v": {}, "wq_st": {}}

            def issue_ld(j):
                if j >= n_rt:
                    return
                if j - R in st["used_v"]:
                    E.wait("s", "vs", st["used_v"][j - R])
                elif j < R:
                    E.wait("s", "vs", dq_state["prev_used_v"])
                st["ld"][j] = E.dma("sd", raw_sl[j % R],
                                    w_ap[j * 128:(j + 1) * 128, :])

            def unit(rt):
                if rt == 0:
                    for j in range(min(R, n_rt)):
                        issue_ld(j)
                else:
                    issue_ld(rt + R - 1)
                wv = raw_sl[rt % R].rearrange("p (n k) -> p n k", k=32)
                E.wtok("v", st["ld"][rt])
                E.chain("v", "tensor_reduce", wsca[:, 0:nblk][:, :, None], wv,
                        AXX, ALU.max, apply_absolute_value=True)
                E.chain("v", "tensor_scalar", wsca[:, nblk:2 * nblk],
                        wsca[:, 0:nblk], 1.0 / QW, 1e-12, ALU.mult, ALU.max)
                E.chain("v", "reciprocal", wsca[:, 2 * nblk:3 * nblk],
                        wsca[:, nblk:2 * nblk])
                rs_b = wsca[:, 2 * nblk:3 * nblk][:, :, None].broadcast_to(
                    [128, nblk, 32])
                s_b = wsca[:, nblk:2 * nblk][:, :, None].broadcast_to(
                    [128, nblk, 32])
                used = E.chain("v", "tensor_tensor",
                               fscr[:, 0:cols].rearrange("p (n k) -> p n k", k=32),
                               wv, rs_b, ALU.mult)
                st["used_v"][rt] = used
                E.chain("v", "tensor_scalar", fscr[:, 0:cols], fscr[:, 0:cols],
                        MAGIC, MAGIC, ALU.add, ALU.subtract)
                E.chain("v", "tensor_scalar", fscr[:, 0:cols], fscr[:, 0:cols],
                        QW, -QW - 1.0, ALU.min, ALU.max)
                if rt - 2 in st["wq_st"]:
                    E.wtok("v", st["wq_st"][rt - 2])
                elif rt < 2:
                    for tok in dq_state["prev_wq"]:
                        E.wtok("v", tok)
                wq_v = E.chain("v", "tensor_tensor",
                               wq_sl[rt % 2].rearrange("p (n k) -> p n k", k=32),
                               fscr[:, 0:cols].rearrange("p (n k) -> p n k", k=32),
                               s_b, ALU.mult)
                E.wait("sync", "vs", wq_v)
                st["wq_st"][rt] = E.dma(
                    "sync", wq_d_ap[rt * 128:(rt + 1) * 128, :], wq_sl[rt % 2])
                if rt == n_rt - 1:
                    dq_state["prev_used_v"] = used
                    dq_state["prev_wq"] = [st["wq_st"][n_rt - 2],
                                           st["wq_st"][n_rt - 1]]

            def finish():
                for tok in st["wq_st"].values():
                    E.wtok("sync", tok)
                E.wtok("sync", dq_state["prev_stage"])
                if dq_state["ts_guard"] is not None:
                    E.wait("sync", "ts", dq_state["ts_guard"])
                    dq_state["ts_guard"] = None
                wtr = dq_wtr(nct * rows)
                tr_lds = [E.dma("sync", wtr[:, ct * rows:(ct + 1) * rows],
                                wq_d_ap[:, ct * 128:(ct + 1) * 128],
                                transpose=True) for ct in range(nct)]
                for tok in tr_lds:
                    E.wtok("sync", tok)
                if nct == 1:
                    stage = E.dma("sync", wT_in_ap, wtr[:, 0:rows])
                else:
                    stage = E.dma("sync",
                                  wT_in_ap.rearrange("(ct p) d -> p ct d", p=128),
                                  wtr[:, 0:nct * rows].rearrange(
                                      "p (ct r) -> p ct r", ct=nct))
                dq_state["prev_stage"] = stage
                dq_ag[name + "_st"] = stage
                if name != "down":
                    E.wtok("g", stage)
                    dq_ag[name] = E.op("g", "collective_compute", "AllGather",
                                       ALU.bypass, RG8, [g_in], [g_out])

            return [lambda rt=rt: unit(rt) for rt in range(n_rt)] + [finish]

        dq_units = (mk_dq("up", wup_sh, IC, 128, wqup_d, wupT_in,
                          wupT_in, wupT_g)
                    + mk_dq("gate", wgate_sh, IC, 128, wqgate_d, wgateT_in,
                            wgateT_in, wgateT_g)
                    + mk_dq("down", wdown_sh, DC, WDSH, wqdown_d, wdownT_in,
                            wdownT_in, wdownT_g))
        dq_pos = [0]

        def drain_dq(n):
            k = 0
            while dq_pos[0] < len(dq_units) and k < n:
                dq_units[dq_pos[0]]()
                dq_pos[0] += 1
                k += 1

        if phases <= 3:
            dummy = E.dma("sync", y_out.rearrange("b (c p) d -> b c p d", p=128)[0, 0],
                          scr1[:])
            E.wtok("sync", dummy)
            emit_blocks()
            return nc

        if phases <= 4:
            dummy = E.dma("sync", y_out.rearrange("b (c p) d -> b c p d", p=128)[0, 0],
                          scr1[:])
            E.wtok("sync", dummy)
            emit_blocks()
            return nc

        xres_t = x_in.rearrange("b (c p) d -> b c p d", p=128)
        xatt_ld = {}
        batch_mm_done = {}
        att_ev = {}
        ln2_prev_ssq = [prev_ssq]
        ln2_red_prev = [0]
        dq_state["ts_guard"] = E.val("ts")
        dh0_done = {}
        for b in range(B):
            if b >= 1:
                E.wait("sync", "ts", dh0_done[b - 1])
            E.wait("sync", "gs", xln_gath_g[b])
            src = xln_g[b].rearrange("r e p c d -> e p r c d")
            xatt_ld[(b, 0)] = E.dma("sync", xatt4[0][:], src[0])
            if b >= 1:
                E.wait("sync", "ts", batch_mm_done[b - 1])
            xatt_ld[(b, 1)] = E.dma("sync", xatt4[1][:], src[1])
            for dh in range(2):
                buf = xatt[dh]
                k = b * 2 + dh
                for i in range(SC):
                    gi = k * SC + i
                    bank = pbank[gi % 6]
                    if gi == 0:
                        E.wait("t", "vs", clip_v[NJ - 1])
                    if gi >= 6:
                        E.wait("t", "vs", att_ev[gi - 6])
                    E.wtok("t", xatt_ld[(b, dh)])
                    for jc in range(NJ - 1):
                        E.raw("t", "matmul", bank[:], E_sb[:, jc, i * 128:(i + 1) * 128],
                              buf[:, jc, :], start=(jc == 0), stop=False)
                    mm_t = E.op("t", "matmul", bank[:],
                                E_sb[:, NJ - 1, i * 128:(i + 1) * 128],
                                buf[:, NJ - 1, :], start=(NJ == 1), stop=True)
                    slot = gi % 2
                    if gi >= 2:
                        E.wait("s", "vs", att_ev[gi - 2])
                    xr_ld = E.dma("sd", xres[:, slot, :],
                                  xres_t[b, i][:, dh * 512:(dh + 1) * 512])
                    E.wait("v", "ts", mm_t)
                    E.wtok("v", xr_ld)
                    if gi == 0:
                        E.wait("v", "vs", rsv)
                    att_ev[gi] = E.op("v", "scalar_tensor_tensor",
                                      y1[:, b * SC + i, dh * 512:(dh + 1) * 512],
                                      bank[:], recip_s[:, i:i + 1], xres[:, slot, :],
                                      ALU.mult, ALU.add)
                    drain_dq(3)
                if dh == 0:
                    dh0_done[b] = E.val("ts")
            batch_mm_done[b] = E.val("ts")
            for c2 in range(SC):
                t = b * SC + c2
                E.wait("v", "vs", att_ev[(b * 2 + 1) * SC + c2])
                E.wait("s", "vs", att_ev[(b * 2 + 1) * SC + c2])
                E.wait("s", "vs", ln2_red_prev[0])
                _, ln2_prev_ssq[0], x2_v = ln_tile(
                    y1[:, t, :], scr2[:], ln2_prev_ssq[0],
                    mean_ap=ln2_mean[:, t:t + 1], isd_ap=ln2_isd[:, t:t + 1])
                E.wait("v", "ss", x2_v)
                ln2_red_prev[0] = E.chain("v", "tensor_reduce", absmax[:, 1:2],
                                          scr2[:], AXX, ALU.max,
                                          apply_absolute_value=True)
                if t == 0:
                    E.chain("v", "tensor_copy", absmax[:, 0:1], absmax[:, 1:2])
                else:
                    E.chain("v", "tensor_tensor", absmax[:, 0:1],
                            absmax[:, 0:1], absmax[:, 1:2], ALU.max)
        attn_all_v = att_ev[2 * B * SC - 1]
        drain_dq(999)

        if phases <= 5:
            dummy = E.dma("sync", y_out.rearrange("b (c p) d -> b c p d", p=128)[0, 0],
                          scr1[:])
            E.wtok("sync", dummy)
            emit_blocks()
            return nc

        ln2_all_v = E.val("vs")

        def barrier_maxes(cols, bar_in_ap, bar_out_ap):
            n = len(cols)
            E.wait("t", "vs", E.val("vs"))
            E.wtok("t", ld_idf)
            rd_v = 0
            for ii, cc in enumerate(cols):
                if ii > 0:
                    E.wait("t", "vs", rd_v)
                tr = E.op("t", "transpose", paux[0:1, 0:128],
                          absmax[:, cc:cc + 1], id_f[:])
                E.wait("v", "ts", tr)
                rd_v = E.chain("v", "reduce_max", bar_st[:, 8 + ii:9 + ii],
                               paux[0:1, 0:128], AXX)
            E.wait("sync", "vs", rd_v)
            st = E.dma("sync", bar_in_ap[:, 0:n], bar_st[:, 8:8 + n])
            E.wtok("g", st)
            gg = E.op("g", "collective_compute", "AllReduce", ALU.max,
                      RG8, [bar_in_ap[:, 0:n]], [bar_out_ap[:, 0:n]])
            E.wait("sync", "gs", gg)
            return E.dma("sync", bar_st[:, 0:n], bar_out_ap[:, 0:n])

        def bcast(src_col, dst_col, n):
            E.wait("t", "vs", E.val("vs"))
            mmv = E.op("t", "matmul", paux[:, 0:n], ones_f[:],
                       bar_st[:, src_col:src_col + n], start=True, stop=True)
            E.wait("v", "ts", mmv)
            return E.chain("v", "tensor_copy", scal_bc[:, dst_col:dst_col + n],
                           paux[:, 0:n])

        b1_rd = barrier_maxes([0], bars_in[0], bars_out[0])
        E.wtok("g", dq_ag["down_st"])
        dq_ag["down"] = E.op("g", "collective_compute", "AllGather",
                             ALU.bypass, RG8, [wdownT_in], [wdownT_g])
        E.wtok("v", b1_rd)
        E.chain("v", "tensor_scalar", bar_st[:, 2:3], bar_st[:, 0:1],
                1.0 / QMAX, 1e-12, ALU.mult, ALU.max)
        E.chain("v", "reciprocal", bar_st[:, 3:4], bar_st[:, 2:3])
        bc1_v = bcast(2, 0, 2)

        E.wait("v", "vs", bc1_v)
        E.wait("v", "ss", E.val("ss"))
        E.chain("v", "tensor_tensor", sc_xq[:], ln2_isd[:],
                scal_bc[:, 1:2].broadcast_to([128, NT]), ALU.mult)
        E.chain("v", "tensor_tensor", ln2_mean[:], ln2_mean[:], sc_xq[:],
                ALU.mult)
        pre_v = E.chain("v", "tensor_scalar", ln2_mean[:], ln2_mean[:],
                        -1.0, MAGIC, ALU.mult, ALU.add)
        xq_d_t = xq_d.rearrange("(t p) d -> t p d", p=128)
        xq_tile_st = {}
        xq_s = {}
        xq_vv = {}
        E.wait("s", "vs", pre_v)
        for t in range(NT):
            buf = scr1 if t % 2 == 0 else scr2
            if t >= 2:
                E.wait("s", "vs", xq_vv[t - 2])
            xq_s[t] = E.op("s", "activation", buf[:], y1[:, t, :],
                           AF.Identity, scale=sc_xq[:, t:t + 1],
                           bias=ln2_mean[:, t:t + 1])
            E.wait("v", "ss", xq_s[t])
            slot = t % 2
            if t >= 2:
                E.wtok("v", xq_tile_st[t - 2])
            xq_v = E.chain("v", "tensor_scalar", xq_ring[:, slot, :], buf[:],
                           MAGIC, scal_bc[:, 0:1], ALU.subtract, ALU.mult)
            xq_vv[t] = xq_v
            E.wait("sync", "vs", xq_v)
            xq_tile_st[t] = E.dma("sync", xq_d_t[t], xq_ring[:, slot, :])
        for t in range(NT):
            E.wtok("sync", xq_tile_st[t])
        xqT_lds = []
        for dc in range(DC):
            xqT_lds.append(E.dma("sync", xqT[:, dc, :],
                                 xq_d[:, dc * 128:(dc + 1) * 128], transpose=True))

        if phases <= 6:
            dummy = E.dma("sync", y_out.rearrange("b (c p) d -> b c p d", p=128)[0, 0],
                          scr1[:])
            E.wtok("sync", dummy)
            emit_blocks()
            return nc

        up_spT = up_sp.rearrange("(ic p) t -> ic p t", p=128)
        gate_spT = gate_sp.rearrange("(ic p) t -> ic p t", p=128)
        tp_spT = tp_sp.rearrange("(ic p) t -> ic p t", p=128)
        wupT_v = wupT_g.rearrange("r p i -> p r i")
        wgateT_v = wgateT_g.rearrange("r p i -> p r i")
        ug_spill = {}
        wb_mm_done = {}
        first_max = True
        E.wait("sync", "gs", dq_ag["up"])
        E.wait("sync", "gs", dq_ag["gate"])
        ev_s = 0
        ev_v = 0
        last_red_v = 0
        sp_done_for_stage = None
        ev_ring = {}
        _P10N = int(_os.environ.get("P10N", "999"))
        for ic in range(min(IC, _P10N)):
            wb = wbuf[ic % 2]
            if ic >= 2:
                E.wait("sync", "ts", wb_mm_done[ic - 2])
            l1 = E.dma("sync", wb[:, 0:DC, :],
                       wupT_v[:, :, ic * 128:(ic + 1) * 128])
            l2 = E.dma("sync", wb[:, DC:2 * DC, :],
                       wgateT_v[:, :, ic * 128:(ic + 1) * 128])
            E.wtok("t", l1)
            E.wtok("t", l2)
            if ic == 0:
                for tok in xqT_lds:
                    E.wtok("t", tok)
            NQ = T // 512
            for tc in range(NQ):
                gstep = ic * NQ + tc
                bU = pbank[(2 * gstep) % 6]
                bG = pbank[(2 * gstep + 1) % 6]
                if gstep >= 3:
                    E.wait("t", "ss", ev_ring[gstep - 3][0])
                    E.wait("t", "vs", ev_ring[gstep - 3][1])
                mm_t = None
                for dc in range(DC):
                    last = (dc == DC - 1)
                    E.raw("t", "matmul", bU[:], wb[:, dc, :],
                          xqT[:, dc, tc * 512:(tc + 1) * 512],
                          start=(dc == 0), stop=last)
                    if not last:
                        E.raw("t", "matmul", bG[:], wb[:, DC + dc, :],
                              xqT[:, dc, tc * 512:(tc + 1) * 512],
                              start=(dc == 0), stop=False)
                    else:
                        mm_t = E.op("t", "matmul", bG[:],
                                    wb[:, DC + dc, :],
                                    xqT[:, dc, tc * 512:(tc + 1) * 512],
                                    start=(dc == 0), stop=True)
                E.wait("s", "ts", mm_t)
                E.wait("v", "ts", mm_t)
                if tc == 0:
                    if sp_done_for_stage:
                        E.wtok("s", sp_done_for_stage[0])
                        E.wtok("s", sp_done_for_stage[1])
                        E.wtok("v", sp_done_for_stage[0])
                        E.wtok("v", sp_done_for_stage[1])
                    E.wait("s", "vs", last_red_v)
                    E.wait("v", "ss", ev_s)
                sc_c = E.op("s", "activation",
                            ug_stage[0][:, tc * 512:(tc + 1) * 512],
                            bU[:], AF.Copy)
                ve_c = E.op("v", "tensor_copy",
                            ug_stage[1][:, tc * 512:(tc + 1) * 512],
                            bG[:])
                ev_ring[gstep] = (sc_c, ve_c)
                ev_s = sc_c
                ev_v = ve_c
            E.wait("v", "ss", ev_s)
            E.chain("v", "tensor_reduce", wsca[:, ic:ic + 1], ug_stage[0][:],
                    AXX, ALU.max, apply_absolute_value=True)
            last_red_v = E.chain("v", "tensor_reduce", wsca[:, IC + ic:IC + ic + 1],
                                 ug_stage[1][:], AXX, ALU.max,
                                 apply_absolute_value=True)
            wb_mm_done[ic] = E.val("ts")
            if _os.environ.get("P10SPILL", "1") == "0":
                ug_spill[ic] = None
                continue
            E.wait("s", "ss", ev_s)
            s1d = E.dma("sd", up_spT[ic], ug_stage[0][:])
            s2d = E.dma("sd", gate_spT[ic], ug_stage[1][:])
            ug_spill[ic] = (s1d, s2d)
            sp_done_for_stage = (s1d, s2d)

        if _os.environ.get("P10B", "1") == "0":
            dummy = E.dma("sync", y_out.rearrange("b (c p) d -> b c p d", p=128)[0, 0],
                          scr1[:])
            E.wtok("sync", dummy)
            emit_blocks()
            return nc
        E.chain("v", "tensor_reduce", absmax[:, 2:3], wsca[:, 0:IC],
                AXX, ALU.max)
        E.chain("v", "tensor_reduce", absmax[:, 3:4], wsca[:, IC:2 * IC],
                AXX, ALU.max)
        tp_st = {}
        p1_ss = {}
        k3_vv = {}
        sig_ss = {}
        stt_vv = {}
        g_lds = {}

        def f5_ld(j):
            if j >= IC:
                return
            E.wtok("sync", ug_spill[j][1])
            if j >= 3:
                E.wait("sync", "ss", p1_ss[j - 3])
            g_lds[j] = E.dma("sync", g_rd[j % 3][:], gate_spT[j])

        for _ic in range(max(0, IC - 4), IC):
            E.wtok("sync", ug_spill[_ic][0])
            E.wtok("sync", ug_spill[_ic][1])
        E.wait("sync", "vs", E.val("vs"))
        for j in range(3):
            f5_ld(j)
        b2_rd = barrier_maxes([2, 3], bars_in[1], bars_out[1])
        E.wtok("v", b2_rd)
        E.chain("v", "tensor_scalar", bar_st[:, 2:3], bar_st[:, 0:1],
                1.0 / QMAX, 1e-12, ALU.mult, ALU.max)
        E.chain("v", "reciprocal", bar_st[:, 3:4], bar_st[:, 2:3])
        E.chain("v", "tensor_scalar", bar_st[:, 4:5], bar_st[:, 1:2],
                1.0 / QMAX, 1e-12, ALU.mult, ALU.max)
        E.chain("v", "reciprocal", bar_st[:, 5:6], bar_st[:, 4:5])
        E.chain("v", "tensor_scalar_mul", bar_st[:, 6:7], bar_st[:, 4:5],
                -MAGIC)
        bc2_v = bcast(2, 2, 5)

        if phases <= 7:
            dummy = E.dma("sync", y_out.rearrange("b (c p) d -> b c p d", p=128)[0, 0],
                          scr1[:])
            E.wtok("sync", dummy)
            emit_blocks()
            return nc

        E.wait("v", "vs", bc2_v)
        E.wait("s", "vs", bc2_v)
        for it in range(IC + 2):
            if it < IC:
                ic = it
                rb = g_rd[ic % 3]
                gq = gq_sb[ic % 3]
                E.wtok("s", g_lds[ic])
                if ic >= 3:
                    E.wait("s", "vs", stt_vv[ic - 3])
                p1_ss[ic] = E.op("s", "activation", gq[:], rb[:], AF.Copy,
                                 scale=scal_bc[:, 5:6], bias=MAGIC)
                f5_ld(ic + 3)
            if 1 <= it < IC + 1:
                ic = it - 1
                gq = gq_sb[ic % 3]
                sg = sg_sb[ic % 3]
                if ic >= 3:
                    E.wait("s", "vs", stt_vv[ic - 3])
                E.op("s", "activation", sg[:], gq[:], AF.Sigmoid,
                     scale=scal_bc[:, 4:5], bias=scal_bc[:, 6:7])
                sig_ss[ic] = E.op("s", "activation", sg[:], sg[:], AF.Copy,
                                  scale=65536.0, bias=MAGIC)
            if it >= 2:
                ic = it - 2
                gq = gq_sb[ic % 3]
                sg = sg_sb[ic % 3]
                tb = tp_sb[ic % 2]
                E.wait("v", "ss", sig_ss[ic])
                E.chain("v", "tensor_scalar", sg[:], sg[:],
                        MAGIC, 65535.0, ALU.subtract, ALU.min)
                if ic >= 2:
                    E.wtok("v", tp_st[ic - 2])
                stt_vv[ic] = E.chain("v", "scalar_tensor_tensor", tb[:],
                                     gq[:], MAGIC, sg[:],
                                     ALU.subtract, ALU.mult)
                E.chain("v", "tensor_reduce", wsca[:, 2 * IC + ic:2 * IC + ic + 1],
                        tb[:], AXX, ALU.max, apply_absolute_value=True)
                E.wait("sync", "vs", stt_vv[ic])
                tp_st[ic] = E.dma("sync", tp_spT[ic], tb[:])

        E.chain("v", "tensor_reduce", absmax[:, 4:5], wsca[:, 2 * IC:3 * IC],
                AXX, ALU.max)
        t2_st = {}
        p1b_ss = {}
        p2b_ss = {}
        k4_vv = {}
        q2b_gg = {}
        stt2_vv = {}
        tp_lds = {}
        up_lds = {}

        def f7_ld(j):
            if j >= IC:
                return
            E.wtok("sync", tp_st[j])
            E.wtok("sync", ug_spill[j][0])
            if j == 0:
                E.wtok("sync", dq_ag["down_st"])
            if j >= 3:
                E.wait("sync", "ss", p1b_ss[j - 3])
            else:
                E.wait("sync", "ss", p1_ss[IC - 3 + j])
            tp_lds[j] = E.dma("sync", g_rd[j % 3][:], tp_spT[j])
            if j >= 3:
                E.wait("sync", "ss", p2b_ss[j - 3])
            up_lds[j] = E.dma("sync", u_rd[j % 3][:], up_spT[j])

        for j in range(3):
            f7_ld(j)
        b3_rd = barrier_maxes([4], bars_in[2], bars_out[2])
        E.wtok("v", b3_rd)
        E.chain("v", "tensor_scalar", bar_st[:, 7:8], bar_st[:, 0:1],
                bar_st[0:1, 4:5], 1.0 / 65536.0 / QMAX, ALU.mult, ALU.mult)
        E.chain("v", "tensor_scalar_max", bar_st[:, 7:8], bar_st[:, 7:8],
                1e-12)
        E.chain("v", "reciprocal", bar_st[:, 3:4], bar_st[:, 7:8])
        E.chain("v", "tensor_scalar", bar_st[:, 3:4], bar_st[:, 3:4],
                bar_st[0:1, 4:5], 1.0 / 65536.0, ALU.mult, ALU.mult)
        bc3_v = bcast(3, 7, 1)

        if phases <= 8:
            dummy = E.dma("sync", y_out.rearrange("b (c p) d -> b c p d", p=128)[0, 0],
                          scr1[:])
            E.wtok("sync", dummy)
            emit_blocks()
            return nc

        E.wait("v", "vs", bc3_v)
        E.wait("s", "vs", bc3_v)
        for it in range(IC + 1):
            if it < IC:
                ic = it
                rb = g_rd[ic % 3]
                ub = u_rd[ic % 3]
                gq = gq_sb[ic % 3]
                sg = sg_sb[ic % 3]
                E.wtok("s", tp_lds[ic])
                if ic >= 3:
                    E.wait("s", "vs", stt2_vv[ic - 3])
                p1b_ss[ic] = E.op("s", "activation", gq[:], rb[:], AF.Copy,
                                  scale=scal_bc[:, 7:8], bias=MAGIC)
                E.wtok("s", up_lds[ic])
                if ic >= 3:
                    E.wait("s", "vs", stt2_vv[ic - 3])
                p2b_ss[ic] = E.op("s", "activation", sg[:], ub[:], AF.Copy,
                                  scale=scal_bc[:, 3:4], bias=MAGIC)
                f7_ld(ic + 3)
                E.wait("v", "ss", p1b_ss[ic])
                k4_vv[ic] = E.chain("v", "tensor_scalar", gq[:], gq[:],
                                    MAGIC, None, ALU.subtract)
            if it >= 1:
                ic = it - 1
                gq = gq_sb[ic % 3]
                sg = sg_sb[ic % 3]
                tb = tp_sb[ic % 2]
                E.wait("v", "ss", p2b_ss[ic])
                if ic >= 2:
                    E.wtok("v", t2_st[ic - 2])
                stt2_vv[ic] = E.chain("v", "scalar_tensor_tensor", tb[:],
                                      sg[:], MAGIC, gq[:],
                                      ALU.subtract, ALU.mult)
                E.chain("v", "tensor_reduce",
                        wsca[:, 3 * IC + ic:3 * IC + ic + 1],
                        tb[:], AXX, ALU.max, apply_absolute_value=True)
                E.wait("sync", "vs", stt2_vv[ic])
                t2_st[ic] = E.dma("sync", gate_spT[ic], tb[:])

        E.chain("v", "tensor_reduce", absmax[:, 6:7], wsca[:, 3 * IC:4 * IC],
                AXX, ALU.max)
        t2_view = gate_sp.rearrange("(ic p) t -> p ic t", p=128)
        t2_lds = {}
        o_v = {}

        def p16_ld(j):
            if j >= NT:
                return
            if j == 0:
                for _tok in t2_st.values():
                    E.wtok("sync", _tok)
            if j >= 2:
                E.wait("sync", "vs", o_v[j - 2])
            t2_lds[j] = E.dma("sync", t2_rd[j % 2][:],
                              t2_view[:, :, j * 128:(j + 1) * 128])

        for j in range(2):
            p16_ld(j)
        b4_rd = barrier_maxes([6], bars_in[3], bars_out[3])
        E.wtok("v", b4_rd)
        E.chain("v", "tensor_scalar", bar_st[:, 6:7], bar_st[:, 7:8],
                bar_st[0:1, 2:3], None, ALU.mult)
        E.chain("v", "tensor_scalar", bar_st[:, 4:5], bar_st[:, 0:1],
                bar_st[0:1, 6:7], 1.0 / QMAX, ALU.mult, ALU.mult)
        E.chain("v", "tensor_scalar_max", bar_st[:, 4:5], bar_st[:, 4:5],
                1e-12)
        E.chain("v", "reciprocal", bar_st[:, 5:6], bar_st[:, 4:5])
        E.chain("v", "tensor_scalar", bar_st[:, 5:6], bar_st[:, 5:6],
                bar_st[0:1, 6:7], None, ALU.mult)
        bc4_v = bcast(4, 8, 2)

        E.wait("sync", "gs", dq_ag["down"])
        wdT_ld = E.dma("sync", wdT_sb[:],
                       wdownT_g.rearrange("r w d -> (r w) d").rearrange(
                           "(ic p) d -> p ic d", p=128))

        if phases <= 9:
            dummy = E.dma("sync", y_out.rearrange("b (c p) d -> b c p d", p=128)[0, 0],
                          scr1[:])
            E.wtok("sync", dummy)
            emit_blocks()
            return nc

        dn_done = {}
        fin_done = {}
        out_st = {}
        E.wait("v", "vs", bc4_v)
        E.wait("s", "vs", bc4_v)
        for tc in range(NT + 1):
            if tc < NT:
                rb = t2_rd[tc % 2]
                ob = o_buf[tc % 2]
                rv = rb[:].rearrange("p a q -> p (a q)")
                ov = ob[:].rearrange("p a q -> p (a q)")
                E.wtok("s", t2_lds[tc])
                if tc >= 2:
                    E.wait("s", "vs", o_v[tc - 2])
                p1_s = E.op("s", "activation", rv, rv, AF.Copy,
                            scale=scal_bc[:, 9:10], bias=MAGIC)
                E.wait("v", "ss", p1_s)
                o_v[tc] = E.chain("v", "tensor_scalar", ov, rv,
                                  MAGIC, scal_bc[:, 8:9], ALU.subtract, ALU.mult)
                p16_ld(tc + 2)
                E.wait("t", "vs", o_v[tc])
                E.wtok("t", wdT_ld)
                bk = 2 * (tc % 2)
                if tc >= 2:
                    E.wait("t", "vs", fin_done[tc - 2])
                dn_t = None
                for d2c in range(2):
                    for icx in range(IC):
                        last = (d2c == 1) and (icx == IC - 1)
                        if not last:
                            E.raw("t", "matmul", pbank[bk + d2c][:], ob[:, icx, :],
                                  wdT_sb[:, icx, d2c * 512:(d2c + 1) * 512],
                                  start=(icx == 0), stop=(icx == IC - 1))
                        else:
                            dn_t = E.op("t", "matmul", pbank[bk + d2c][:],
                                        ob[:, icx, :],
                                        wdT_sb[:, icx, d2c * 512:(d2c + 1) * 512],
                                        start=(icx == 0), stop=True)
                dn_done[tc] = dn_t
            if tc >= 1:
                tp = tc - 1
                bkp = 2 * (tp % 2)
                slot = tp % 2
                E.wait("v", "ts", dn_done[tp])
                if tp >= 2:
                    E.wtok("v", out_st[tp - 2])
                E.chain("v", "tensor_add", out_sb[:, slot, 0:512], pbank[bkp][:],
                        y1[:, tp, 0:512])
                fin_done[tp] = E.chain("v", "tensor_add",
                                       out_sb[:, slot, 512:1024],
                                       pbank[bkp + 1][:], y1[:, tp, 512:1024])
                b, c = tp // SC, tp % SC
                E.wait("sync", "vs", fin_done[tp])
                out_st[tp] = E.dma("sync",
                                   y_out.rearrange("b (c p) d -> b c p d", p=128)[b, c],
                                   out_sb[:, slot, :])

        E.wtok("sync", out_st[NT - 1])

        emit_blocks()
        return nc
    return nc



def kernel(L0_attn_x, L0_ffn_w_gate, L0_ffn_w_up, L0_ffn_w_down):
    x = np.asarray(L0_attn_x, dtype=np.float32)
    wg = np.asarray(L0_ffn_w_gate, dtype=np.float32)
    wu = np.asarray(L0_ffn_w_up, dtype=np.float32)
    wd = np.asarray(L0_ffn_w_down, dtype=np.float32)
    B, L, D = x.shape
    I = wu.shape[0]
    n = 8
    S = L // n
    nc = build_nc(B=B, L=L, D=D, H=16, I=I, n_cores=n)
    ident_bf = np.eye(128, dtype=ml_dtypes.bfloat16)
    ident_f = np.eye(128, dtype=np.float32)
    in_maps = []
    for c in range(n):
        in_maps.append({
            "x": np.ascontiguousarray(x[:, S * c:S * (c + 1), :]),
            "wup_sh": np.ascontiguousarray(wu[:, (D // n) * c:(D // n) * (c + 1)]),
            "wgate_sh": np.ascontiguousarray(wg[:, (D // n) * c:(D // n) * (c + 1)]),
            "wdown_sh": np.ascontiguousarray(wd[:, (I // n) * c:(I // n) * (c + 1)]),
            "ident_bf": ident_bf,
            "ident_f": ident_f,
        })
    res = run_bass_kernel_spmd(nc, in_maps, core_ids=list(range(n)))
    return np.concatenate([res.results[c]["y"] for c in range(n)], axis=1)
